# version 21
# baseline (speedup 1.0000x reference)
import numpy as np
import concourse.bass as bass
import concourse.mybir as mybir
from concourse.bass_utils import run_bass_kernel_spmd

F32 = mybir.dt.float32
BF16 = mybir.dt.bfloat16
AF = mybir.ActivationFunctionType
ALU = mybir.AluOpType
AX = mybir.AxisListType

NCORES = 8
S, D, NH, HD = 2048, 1024, 16, 64
NT, NKT = 16, 8
NE, INTER = 8, 4096
EPS = 1e-5
_DEBUG = False


def build_nc():
    nc = bass.Bass("TRN2", target_bir_lowering=False)

    xin = nc.dram_tensor("xin", [S, D], F32, kind="ExternalInput").ap()
    hm_d = nc.dram_tensor("hm", [NH], F32, kind="ExternalInput").ap()
    qkvw = nc.dram_tensor("qkvw", [3 * D, D], F32, kind="ExternalInput").ap()
    qkvb = nc.dram_tensor("qkvb", [3 * D], F32, kind="ExternalInput").ap()
    fcw = nc.dram_tensor("fcw", [D, D], F32, kind="ExternalInput").ap()
    fcb = nc.dram_tensor("fcb", [D], F32, kind="ExternalInput").ap()
    wg = nc.dram_tensor("wg", [NE, D], F32, kind="ExternalInput").ap()
    w1 = nc.dram_tensor("w1", [NE, D, INTER], F32, kind="ExternalInput").ap()
    b1 = nc.dram_tensor("b1", [NE, INTER], F32, kind="ExternalInput").ap()
    w2 = nc.dram_tensor("w2", [NE, INTER, D], F32, kind="ExternalInput").ap()
    b2 = nc.dram_tensor("b2", [NE, D], F32, kind="ExternalInput").ap()
    xout = nc.dram_tensor("xout", [S, D], F32, kind="ExternalOutput").ap()
    oddb = nc.dram_tensor("oddb", [64, 2048], BF16, kind="Internal").ap()
    dbg = {}
    if _DEBUG:
        U16 = mybir.dt.uint16
        dbg["x16a"] = nc.dram_tensor("dbg_x16a", [128, NT * 1024], U16, kind="ExternalOutput").ap()
        dbg["gates"] = nc.dram_tensor("dbg_gates", [128, 128], F32, kind="ExternalOutput").ap()
        dbg["xnfm1"] = nc.dram_tensor("dbg_xnfm1", [128, NKT * 2048], U16, kind="ExternalOutput").ap()
        dbg["qha"] = nc.dram_tensor("dbg_qha", [128, 2048], U16, kind="ExternalOutput").ap()
        dbg["kha"] = nc.dram_tensor("dbg_kha", [128, 2048], U16, kind="ExternalOutput").ap()
        dbg["vsb"] = nc.dram_tensor("dbg_vsb", [128, 1040], U16, kind="ExternalOutput").ap()
        dbg["afm"] = nc.dram_tensor("dbg_afm", [128, 2048], U16, kind="ExternalOutput").ap()
        dbg["const"] = nc.dram_tensor("dbg_const", [128, 64], F32, kind="ExternalOutput").ap()

    x16 = nc.alloc_sbuf_tensor("x16", [128, NT * 1024], BF16)
    xnfm = nc.alloc_sbuf_tensor("xnfm", [128, NKT * 2048], BF16)
    stage = nc.alloc_sbuf_tensor("stage", [128, 3 * 1024], F32)
    TOP = 229184
    A0, B0, C0 = TOP - 98304, TOP - 65536, TOP - 32768

    def _at(name, shape, dtype, off):
        return nc.alloc_sbuf_tensor_at(name, shape, dtype, offset=off)
    h_g = _at("h_g", [128, 16384], BF16, A0)
    afm_t = [_at("afm%d" % p, [128, 2048], BF16, A0 + 4096 * p) for p in range(8)]
    w1v_t = [_at("w1v0", [128, 8192], BF16, B0), _at("w1v1", [128, 8192], BF16, B0 + 16384)]
    qkvT_t = [_at("qkvT0", [128, 3072], BF16, B0), _at("qkvT1", [128, 3072], BF16, B0 + 6144)]
    vsb_t = [_at("vsb0", [128, 1040], BF16, B0 + 12288), _at("vsb1", [128, 1040], BF16, B0 + 14400)]
    PuT_t = _at("PuTt", [128, 2048], BF16, B0 + 16512)
    bcast_t = _at("bcastt", [128, 2048], BF16, B0 + 20608)
    xtok_t = [_at("xtok0", [128, 1024], BF16, B0 + 24704), _at("xtok1", [128, 1024], BF16, B0 + 26752)]
    sqjunk_t = _at("sqjunkt", [128, 1024], BF16, B0 + 28800)
    w2v_t = [_at("w2v0", [128, 8192], BF16, C0), _at("w2v1", [128, 8192], BF16, C0 + 16384)]
    fcT_t = _at("fcTt", [128, 8192], BF16, C0)
    qha_t = _at("qhat", [128, 2048], BF16, C0 + 16384)
    kha_t = _at("khat", [128, 2048], BF16, C0 + 20480)
    tmpodd_t = _at("tmpoddt", [128, 2048], BF16, C0 + 24576)
    lnr = _at("lnr", [65, 1024], F32, C0 + 28672)
    qha_b = nc.alloc_sbuf_tensor("qha_b", [128, 2048], BF16)
    kha_b = nc.alloc_sbuf_tensor("kha_b", [128, 2048], BF16)
    vsb2 = nc.alloc_sbuf_tensor("vsb2", [128, 1040], BF16)
    vsb3 = nc.alloc_sbuf_tensor("vsb3", [128, 1040], BF16)
    rowp = nc.alloc_sbuf_tensor("rowp", [128, 3712], BF16)
    ident = nc.alloc_sbuf_tensor("ident", [128, 128], BF16)
    identF = nc.alloc_sbuf_tensor("identF", [128, 128], F32)
    scrT = nc.alloc_sbuf_tensor("scrT", [128, 128], BF16)
    masks1 = nc.alloc_sbuf_tensor("masks1", [128, 128], BF16)
    masks2 = nc.alloc_sbuf_tensor("masks2", [128, 128], BF16)
    logits = nc.alloc_sbuf_tensor("logits", [128, 128], F32)
    gates = nc.alloc_sbuf_tensor("gates", [128, 128], F32)
    gt2 = nc.alloc_sbuf_tensor("gt2", [128, 16], F32)
    hm_b = nc.alloc_sbuf_tensor("hm_b", [128, 16], F32)
    hm2 = nc.alloc_sbuf_tensor("hm2", [128, 16], F32)
    qb = nc.alloc_sbuf_tensor("qb", [128, 16], F32)
    kb = nc.alloc_sbuf_tensor("kb", [128, 16], F32)
    b1fm = nc.alloc_sbuf_tensor("b1fm", [128, 256], F32)
    WgTb = nc.alloc_sbuf_tensor("WgTb", [128, 64], BF16)
    sums = nc.alloc_sbuf_tensor("sums", [128, 16], F32)
    sumsq = nc.alloc_sbuf_tensor("sumsq", [128, 16], F32)
    meanv = nc.alloc_sbuf_tensor("meanv", [128, 16], F32)
    varv = nc.alloc_sbuf_tensor("varv", [128, 16], F32)
    tmp16 = nc.alloc_sbuf_tensor("tmp16", [128, 16], F32)
    rstd = nc.alloc_sbuf_tensor("rstd", [128, 16], F32)
    negmr = nc.alloc_sbuf_tensor("negmr", [128, 16], F32)
    nm2 = nc.alloc_sbuf_tensor("nm2", [128, 4], F32)
    maxs1 = nc.alloc_sbuf_tensor("maxs1", [128, 16], F32)
    maxs2 = nc.alloc_sbuf_tensor("maxs2", [128, 16], F32)
    e2t = nc.alloc_sbuf_tensor("e2t", [128, 16], F32)
    rden = nc.alloc_sbuf_tensor("rden", [128, 16], F32)
    w2g = nc.alloc_sbuf_tensor("w2g", [128, 16], F32)
    hmrow = nc.alloc_sbuf_tensor("hmrow", [1, 16], F32)

    def attn_fm(p):
        return afm_t[p][:, :]
    w1v = [w1v_t[0][:, :], w1v_t[1][:, :]]
    qkvT = [qkvT_t[0][:, :], qkvT_t[1][:, :]]
    v_sb = [vsb_t[0][:, :], vsb_t[1][:, :]]
    PuT = PuT_t[:, :]
    bcast = bcast_t[:, :]
    xtok = [xtok_t[0][:, :], xtok_t[1][:, :]]
    sqjunk = sqjunk_t[:, :]
    w2v = [w2v_t[0][:, :], w2v_t[1][:, :]]
    fcT = fcT_t[:, :]
    qha = qha_t[:, :]
    kha = kha_t[:, :]
    tmpodd = tmpodd_t[:, :]
    qha2 = [qha_t[:, :], qha_b[:, :]]
    kha2 = [kha_t[:, :], kha_b[:, :]]
    vsb4 = [vsb_t[0][:, :], vsb_t[1][:, :], vsb2[:, :], vsb3[:, :]]
    vb_row = rowp[0:1, 0:1024]
    fcb_row = rowp[0:1, 1024:2048]
    b2_row = rowp[0:1, 2048:3072]
    ones0 = rowp[0:1, 3072:3584]
    hm_bf = rowp[0:1, 3584:3600]
    ones64 = rowp[64:65, 0:512]
    recipr = rowp[64:65, 512:2560]

    PS1 = nc.alloc_psum_tensor("PS1", [128, 1024], F32)
    PS2 = nc.alloc_psum_tensor("PS2", [128, 1024], F32)
    PO = nc.alloc_psum_tensor("PO", [128, 2048], F32)
    pbfLN = [PS1[:, 0:512].bitcast(BF16), PS2[:, 0:512].bitcast(BF16)]
    nt_ps = [PS2[0:65, 960:1024].bitcast(BF16), PO[0:65, 1984:2048].bitcast(BF16)]

    import contextlib
    stack = contextlib.ExitStack()
    sems = {}
    for nm, k in zip(["sK", "sD", "sP", "sT", "sS", "sV", "sB",
                      "sR0", "sR1", "sR2", "sQ0", "sQ1", "sQ2"],
                     ["K", "D", "P", "T", "S", "V", "B",
                      "R0", "R1", "R2", "Q0", "Q1", "Q2"]):
        sems[k] = stack.enter_context(nc.semaphore(nm))
    C = {k: 0 for k in ["K", "D", "P", "T", "S", "V", "B",
                        "R0", "R1", "R2", "Q0", "Q1", "Q2"]}
    snap = {}
    pfree = {r: ("T", 0) for r in ["PS1a", "PS1b", "PS2a", "PS2b", "POa0", "POa1", "POb"]}
    PSQ = [PS1[:, 0:512], PS1[:, 512:1024], PS2[:, 0:512], PS2[:, 512:1024]]
    PSQR = ["PS1a", "PS1b", "PS2a", "PS2b"]

    def IC(instr, s, n=1):
        instr.then_inc(sems[s], n)
        C[s] += n

    def WT(eng, s, v):
        if v > 0:
            eng.wait_ge(sems[s], v)

    def pwait(eng, *regs):
        for r in regs:
            s, v = pfree[r]
            WT(eng, s, v)

    def pset(s, *regs):
        for r in regs:
            pfree[r] = (s, C[s])

    RING = {"ci": 0, "rel": [], "cw": []}

    def slotview(ci, p, n):
        sl = ci % 3
        return stage[0:p, sl * 1024: sl * 1024 + n]

    def ring_dma(sy, src, p, n):
        ci = RING["ci"]
        RING["ci"] += 1
        for (s, v) in RING["rel"][ci - 3] if ci >= 3 else seed_rel:
            WT(sy, s, v)
        key = "R%d" % (ci % 3)
        IC(sy.dma_start(out=slotview(ci, p, n), in_=src), key, 16)
        RING["rel"].append(None)
        RING["cw"].append((key, C[key]))
        return ci

    def rwait(eng, ci):
        key, v = RING["cw"][ci]
        WT(eng, key, v)

    def set_rel(ci, slist):
        RING["rel"][ci] = [(s, C[s]) for s in slist]


    class _FB:
        def __init__(self, name=None):
            pass

        def __enter__(self):
            return self

        def __exit__(self, *a):
            return False

        def sync(self, f):
            f(nc.sync)

        def tensor(self, f):
            f(nc.tensor)

        def vector(self, f):
            f(nc.vector)

        def scalar(self, f):
            f(nc.scalar)

        def gpsimd(self, f):
            f(nc.gpsimd)

    MM = dict(skip_group_check=True)

    with _FB("su_d1") as bk:
        @bk.sync
        def _(sy):
            IC(sy.dma_start(out=stage[0:1, 0:3072], in_=qkvb.rearrange("(a b) -> a b", a=1)), "D", 16)
            IC(sy.dma_start(out=hmrow[0:1, :], in_=hm_d.rearrange("(a b) -> a b", a=1)), "D", 16)
            for h in range(NH):
                IC(sy.dma_start(out=qb[0:64, h:h + 1],
                                in_=qkvb[192 * h:192 * h + 64].rearrange("(p o) -> p o", o=1)), "D", 16)
                IC(sy.dma_start(out=kb[0:64, h:h + 1],
                                in_=qkvb[192 * h + 64:192 * h + 128].rearrange("(p o) -> p o", o=1)), "D", 16)
    snap["d1"] = C["D"]

    with _FB("su_p1") as bk:
        @bk.gpsimd
        def _(pl):
            IC(pl.memset(ident[:], 0.0), "P", 1)
            WT(pl, "P", C["P"])
            pl.affine_select(out=ident[:], in_=ident[:], compare_op=ALU.not_equal,
                             fill=1.0, base=0, pattern=[[-1, 128]], channel_multiplier=1)
            IC(pl.memset(identF[:], 0.0), "P", 1)
            WT(pl, "P", C["P"])
            pl.affine_select(out=identF[:], in_=identF[:], compare_op=ALU.not_equal,
                             fill=1.0, base=0, pattern=[[-1, 128]], channel_multiplier=1)
            pl.memset(scrT[:], 0.0)
            pl.memset(ones0[:], 1.0)
            pl.memset(ones64[:], 1.0)
            pl.memset(qha_t[64:128, :], 0.0)
            pl.memset(qha_b[64:128, :], 0.0)
            IC(pl.memset(kha_t[64:128, :], 0.0), "P", 1)
            WT(pl, "P", C["P"])
            IC(pl.memset(kha_b[64:128, :], 0.0), "P", 1)
            WT(pl, "P", C["P"])
            pl.memset(kha[64:65, :], 1.0)
            pl.memset(kha_b[64:65, :], 1.0)
            for vv in (v_sb[0], v_sb[1], vsb2[:, :], vsb3[:, :]):
                pl.memset(vv.rearrange("p (blk c) -> p blk c", c=65)[:, :, 64:65], 1.0)
            WT(pl, "D", snap["d1"])
            src = stage[0:1, 0:3072].rearrange("p (h c) -> p h c", c=192)[:, :, 128:192]
            pl.tensor_copy(vb_row.rearrange("p (h c) -> p h c", c=64), src)
            IC(pl.tensor_copy(hm_bf[:], hmrow[:]), "P", 1)
    snap["p1"] = C["P"]

    with _FB("su_d2") as bk:
        @bk.sync
        def _(sy):
            WT(sy, "P", snap["p1"])
            IC(sy.dma_start(out=stage[0:1, 0:1024], in_=fcb.rearrange("(a b) -> a b", a=1)), "D", 16)
            for e in range(NE):
                IC(sy.dma_start(out=stage[0:32, 1024 + e * 128:1024 + (e + 1) * 128],
                                in_=b1[e, :].rearrange("(a b) -> a b", b=128)), "D", 16)
            IC(sy.dma_start(out=stage[0:8, 2048:3072], in_=wg[:, :]), "D", 16)
    snap["d2"] = C["D"]

    with _FB("su_p2") as bk:
        @bk.gpsimd
        def _(pl):
            WT(pl, "D", snap["d2"])
            IC(pl.tensor_copy(fcb_row[:], stage[0:1, 0:1024]), "P", 1)
    snap["p2"] = C["P"]

    with _FB("su_t") as bk:
        @bk.tensor
        def _(te):
            WT(te, "D", snap["d2"])
            WT(te, "P", snap["p1"])
            mm = None
            for e in range(NE):
                mm = te.transpose(PS1[:, e * 32:(e + 1) * 32],
                                  stage[0:32, 1024 + e * 128:1024 + (e + 1) * 128], identF[0:32, 0:32])
            for kt in range(NKT):
                mm = te.transpose(PS2[:, kt * 8:(kt + 1) * 8],
                                  stage[0:8, 2048 + kt * 128:2048 + (kt + 1) * 128], identF[0:8, 0:8])
            IC(te.matmul(PO[:, 0:16], ones0[0:1, 0:128], hm_bf[:], start=True, stop=True, **MM), "T", 1)
    snap["sut"] = C["T"]

    with _FB("su_v") as bk:
        @bk.vector
        def _(ve):
            WT(ve, "T", snap["sut"])
            ve.tensor_copy(b1fm[:], PS1[:, 0:256])
            ve.tensor_copy(WgTb[:], PS2[:, 0:64])
            IC(ve.tensor_copy(hm_b[:], PO[:, 0:16]), "V", 1)
            WT(ve, "V", C["V"])
            IC(ve.tensor_tensor(hm2[:], hm_b[:], hm_b[:], op=ALU.mult), "V", 1)
            WT(ve, "V", C["V"])
            IC(ve.tensor_scalar(hm2[:], hm2[:], 0.125, None, op0=ALU.mult), "V", 1)
            WT(ve, "V", C["V"])
            IC(ve.tensor_tensor(qb[0:64, :], qb[0:64, :], hm2[0:64, :], op=ALU.mult), "V", 1)
    snap["suv"] = C["V"]
    seed_rel = [("T", snap["sut"]), ("P", snap["p2"]), ("V", snap["suv"])]

    xcast = [0] * NT
    for t in range(NT):
        with _FB(f"xl{t}") as bk:
            hold = []

            @bk.sync
            def _(sy, t=t):
                hold.append(ring_dma(sy, xin[t * 128:(t + 1) * 128, :], 128, 1024))

            @bk.gpsimd
            def _(pl, t=t):
                ci = hold[0]
                rwait(pl, ci)
                IC(pl.tensor_copy(x16[:, t * 1024:(t + 1) * 1024], slotview(ci, 128, 1024)), "K", 1)
                set_rel(ci, ["K"])
                xcast[t] = C["K"]

    def emit_qkv_chunks(p):
        with _FB(f"qc{p}") as bk:
            @bk.sync
            def _(sy, p=p):
                cis = [ring_dma(sy, qkvw[384 * p + 128 * r:384 * p + 128 * (r + 1), :], 128, 1024)
                       for r in range(3)]
                snap[("qch", p)] = cis

    emit_qkv_chunks(0)

    def emit_layernorm(tag, with_logits):
        def stage_s(t):
            with _FB(f"{tag}s{t}") as bk:
                @bk.scalar
                def _(se, t=t):
                    if tag == "ln1":
                        WT(se, "K", xcast[t])
                    else:
                        WT(se, "V", snap[("fcev", t)])
                    IC(se.activation(sqjunk[:, 0:1024], x16[:, t * 1024:(t + 1) * 1024], AF.Square,
                                     accum_out=sumsq[:, t:t + 1]), "S", 1)
                snap[(f"{tag}sq", t)] = C["S"]

                @bk.vector
                def _(ve, t=t):
                    if tag == "ln1":
                        WT(ve, "K", xcast[t])
                    else:
                        WT(ve, "V", snap[("fcev", t)])
                    IC(ve.reduce_sum(sums[:, t:t + 1], x16[:, t * 1024:(t + 1) * 1024], axis=AX.X), "V", 1)
                snap[(f"{tag}sm", t)] = C["V"]

        def stage_r(t):
            with _FB(f"{tag}r{t}") as bk:
                @bk.vector
                def _(ve, t=t):
                    WT(ve, "V", snap[(f"{tag}sm", t)])
                    ve.tensor_scalar(meanv[:, t:t + 1], sums[:, t:t + 1], 1.0 / D, None, op0=ALU.mult)
                    WT(ve, "S", snap[(f"{tag}sq", t)])
                    ve.tensor_scalar(varv[:, t:t + 1], sumsq[:, t:t + 1], 1.0 / D, None, op0=ALU.mult)
                    IC(ve.tensor_tensor(tmp16[:, t:t + 1], meanv[:, t:t + 1], meanv[:, t:t + 1],
                                        op=ALU.mult), "V", 1)
                    WT(ve, "V", C["V"])
                    ve.tensor_tensor(varv[:, t:t + 1], varv[:, t:t + 1], tmp16[:, t:t + 1], op=ALU.subtract)
                    IC(ve.tensor_scalar(varv[:, t:t + 1], varv[:, t:t + 1], EPS, None, op0=ALU.add), "V", 1)
                snap[(f"{tag}var", t)] = C["V"]

                @bk.scalar
                def _(se, t=t):
                    WT(se, "V", snap[(f"{tag}var", t)])
                    se.activation(tmp16[:, t:t + 1], varv[:, t:t + 1], AF.Ln)
                    IC(se.activation(rstd[:, t:t + 1], tmp16[:, t:t + 1], AF.Exp, scale=-0.5), "S", 1)
                snap[(f"{tag}rstd", t)] = C["S"]

        def stage_n(t):
            with _FB(f"{tag}n{t}") as bk:
                @bk.vector
                def _(ve, t=t):
                    WT(ve, "S", snap[(f"{tag}rstd", t)])
                    IC(ve.scalar_tensor_tensor(negmr[:, t:t + 1], meanv[:, t:t + 1], -1.0,
                                               rstd[:, t:t + 1], op0=ALU.mult, op1=ALU.mult), "V", 1)
                    WT(ve, "V", C["V"])
                    if t >= 2:
                        WT(ve, "T", snap[(f"{tag}tr", t - 2)])
                    IC(ve.tensor_scalar(xtok[t % 2][:, 0:1024], x16[:, t * 1024:(t + 1) * 1024],
                                        rstd[:, t:t + 1], negmr[:, t:t + 1],
                                        op0=ALU.mult, op1=ALU.add), "V", 1)
                snap[(f"{tag}nm", t)] = C["V"]

        def stage_t(t):
            with _FB(f"{tag}t{t}") as bk:
                @bk.tensor
                def _(te, t=t):
                    WT(te, "V", snap[(f"{tag}nm", t)])
                    pwait(te, "PS1a" if t % 2 == 0 else "PS2a")
                    mm = None
                    for kt in range(NKT):
                        mm = te.transpose(pbfLN[t % 2][:, kt * 128:(kt + 1) * 128],
                                          xtok[t % 2][:, kt * 128:(kt + 1) * 128], ident[:])
                    IC(mm, "T", 1)
                snap[(f"{tag}tr", t)] = C["T"]

                @bk.vector
                def _(ve, t=t):
                    WT(ve, "T", snap[(f"{tag}tr", t)])
                    dst = xnfm[:, :].rearrange("p (kt c) -> p kt c", c=2048)[:, :, t * 128:(t + 1) * 128]
                    src = pbfLN[t % 2][:, :].rearrange("p (kt c) -> p kt c", c=128)
                    IC(ve.tensor_copy(dst, src), "V", 1)
                pset("V", "PS1a" if t % 2 == 0 else "PS2a")
                snap[(f"{tag}ev", t)] = C["V"]

        def stage_l(t):
            if with_logits:
                with _FB(f"{tag}l{t}") as bk:
                    @bk.tensor
                    def _(te, t=t):
                        WT(te, "V", snap[(f"{tag}ev", t)])
                        pwait(te, *(("POa0", "POa1") if t % 2 == 0 else ("POb",)))
                        mm = None
                        for kt in range(NKT):
                            mm = te.matmul(PO[:, (t % 2) * 1024 + 0:(t % 2) * 1024 + 8],
                                           xnfm[:, kt * 2048 + t * 128:kt * 2048 + (t + 1) * 128],
                                           WgTb[:, kt * 8:(kt + 1) * 8],
                                           start=(kt == 0), stop=(kt == NKT - 1), **MM)
                        IC(mm, "T", 1)
                    snap[(f"{tag}lm", t)] = C["T"]

                    @bk.vector
                    def _(ve, t=t):
                        WT(ve, "T", snap[(f"{tag}lm", t)])
                        IC(ve.tensor_copy(logits[:, t * 8:(t + 1) * 8],
                                          PO[:, (t % 2) * 1024:(t % 2) * 1024 + 8]), "V", 1)
                    pset("V", *(("POa0", "POa1") if t % 2 == 0 else ("POb",)))
                    snap[(f"{tag}lev", t)] = C["V"]

        stages = [stage_s, stage_r, stage_n, stage_t, stage_l]
        for w in range(NT + len(stages) - 1):
            for si, st in enumerate(stages):
                t = w - si
                if 0 <= t < NT:
                    st(t)
        snap[f"{tag}sq"] = C["S"]

    emit_layernorm("ln1", with_logits=False)
    if _DEBUG:
        with _FB("dbg_xnfm1") as bk:
            @bk.sync
            def _(sy):
                for (s, v) in [("V", C["V"])]:
                    WT(sy, s, v)
                IC(sy.dma_start(out=dbg["xnfm1"], in_=xnfm[:, :].bitcast(mybir.dt.uint16)), "B", 16)

    QKV_DST = {(0, 0): 0, (0, 1): 128, (1, 0): 256, (1, 1): 64, (2, 0): 192, (2, 1): 320}

    def emit_qkv_transpose_r(p, r):
        buf = qkvT[p % 2]
        cis = snap[("qch", p)]
        if True:
            with _FB(f"qt{p}_{r}") as bk:
                @bk.tensor
                def _(te, r=r, p=p):
                    ci = cis[r]
                    rwait(te, ci)
                    if p >= 2:
                        WT(te, "T", snap[("qkdone", p - 2)])
                    regs = ("PS1a", "PS1b") if r % 2 == 0 else ("PS2a", "PS2b")
                    pwait(te, *regs)
                    ps = PS1 if r % 2 == 0 else PS2
                    mm = None
                    for kt in range(NKT):
                        mm = te.transpose(ps[:, kt * 128:(kt + 1) * 128],
                                          slotview(ci, 128, 1024)[:, kt * 128:(kt + 1) * 128],
                                          identF[:])
                    IC(mm, "T", 1)
                    set_rel(ci, ["T"])
                snap[("qtr", p, r)] = C["T"]

                @bk.vector
                def _(ve, r=r, p=p):
                    WT(ve, "T", snap[("qtr", p, r)])
                    ps = PS1 if r % 2 == 0 else PS2
                    last = None
                    for half in range(2):
                        src = ps[:, :].rearrange("p (kt c) -> p kt c", c=128)[:, :, half * 64:(half + 1) * 64]
                        do = QKV_DST[(r, half)]
                        dst = buf.rearrange("p (kt c) -> p kt c", c=384)[:, :, do:do + 64]
                        last = ve.tensor_copy(dst, src)
                    IC(last, "V", 1)
                regs = ("PS1a", "PS1b") if r % 2 == 0 else ("PS2a", "PS2b")
                pset("V", *regs)
        if r == 2:
            snap[("qkv_rdy", p)] = C["V"]

    for _r in range(3):
        emit_qkv_transpose_r(0, _r)
    emit_qkv_chunks(1)
    snap["fcch"] = [None] * 8

    def emit_fc_chunk(et):
        with _FB(f"fcc{et}") as bk:
            @bk.sync
            def _(sy, et=et):
                snap["fcch"][et] = ring_dma(sy, fcw[et * 128:(et + 1) * 128, :], 128, 1024)

    def emit_Av(h, tt):
        p, hi = divmod(h, 2)
        vb = (p % 2) * 2
        with _FB(f"h{h}v{tt}") as bk:
            reg = "PS2a" if tt % 2 == 0 else "PS2b"

            @bk.tensor
            def _(te, tt=tt, p=p, reg=reg):
                if tt == 0:
                    WT(te, "V", snap[("qkv_rdy", p)])
                pwait(te, reg)
                pv = PS2[:, (tt % 2) * 512:(tt % 2) * 512 + 128]
                for kt in range(NKT):
                    te.matmul(pv, xnfm[:, kt * 2048 + tt * 128:kt * 2048 + (tt + 1) * 128],
                              qkvT[p % 2][:, kt * 384 + 256:kt * 384 + 384],
                              start=(kt == 0), stop=False, **MM)
                te.matmul(pv[:, 0:64], ones0[0:1, 0:128],
                          vb_row[0:1, (2 * p) * 64:(2 * p) * 64 + 64], start=False, stop=False, **MM)
                IC(te.matmul(pv[:, 64:128], ones0[0:1, 0:128],
                             vb_row[0:1, (2 * p + 1) * 64:(2 * p + 1) * 64 + 64],
                             start=False, stop=True, **MM), "T", 1)
            snap[("vmm", p, tt)] = C["T"]

            @bk.vector
            def _(ve, tt=tt, p=p, vb=vb):
                WT(ve, "T", snap[("vmm", p, tt)])
                if tt == 0 and p >= 2:
                    WT(ve, "T", snap[("avdone", 2 * p - 3)])
                pv = PS2[:, (tt % 2) * 512:(tt % 2) * 512 + 128]
                ve.tensor_scalar(vsb4[vb][:, tt * 65:tt * 65 + 64], pv[:, 0:64],
                                 hm_b[:, 2 * p:2 * p + 1], None, op0=ALU.mult)
                IC(ve.tensor_scalar(vsb4[vb + 1][:, tt * 65:tt * 65 + 64], pv[:, 64:128],
                                    hm_b[:, 2 * p + 1:2 * p + 2], None, op0=ALU.mult), "V", 1)
            pset("V", reg)
        if tt == NT - 1:
            snap[("vrdy", p)] = C["V"]

    def emit_Aqk(h, c):
        p, hi = divmod(h, 2)
        with _FB(f"h{h}qk{c}") as bk:
            regq = "PS1a" if c % 2 == 0 else "PS1b"
            regk = "PS2a" if c % 2 == 0 else "PS2b"

            @bk.tensor
            def _(te, c=c, h=h, hi=hi, p=p, regq=regq, regk=regk):
                pwait(te, regq, regk)
                pq = PS1[0:64, (c % 2) * 512:(c % 2) * 512 + 512]
                pk = PS2[0:64, (c % 2) * 512:(c % 2) * 512 + 512]
                for kt in range(NKT):
                    te.matmul(pq, qkvT[p % 2][:, kt * 384 + hi * 64:kt * 384 + hi * 64 + 64],
                              xnfm[:, kt * 2048 + c * 512:kt * 2048 + (c + 1) * 512],
                              start=(kt == 0), stop=(kt == NKT - 1), **MM)
                mm = None
                for kt in range(NKT):
                    mm = te.matmul(pk, qkvT[p % 2][:, kt * 384 + 128 + hi * 64:kt * 384 + 128 + hi * 64 + 64],
                                   xnfm[:, kt * 2048 + c * 512:kt * 2048 + (c + 1) * 512],
                                   start=(kt == 0), stop=(kt == NKT - 1), **MM)
                IC(mm, "T", 1)
            snap[("qkmm", h, c)] = C["T"]
            if c == 3 and hi == 1:
                snap[("qkdone", p)] = C["T"]

            @bk.vector
            def _(ve, c=c, h=h, regq=regq, regk=regk):
                WT(ve, "T", snap[("qkmm", h, c)])
                if c == 0 and h >= 2:
                    WT(ve, "T", snap[("etdone", h - 2)])
                if _DEBUG and h == 2 and c == 0:
                    WT(ve, "B", snap["dbgqk"])
                if c == 0:
                    ve.memset(qha2[h % 2][64:65, :], 0.0)
                pq = PS1[0:64, (c % 2) * 512:(c % 2) * 512 + 512]
                pk = PS2[0:64, (c % 2) * 512:(c % 2) * 512 + 512]
                ve.tensor_scalar(qha2[h % 2][0:64, c * 512:(c + 1) * 512], pq,
                                 hm2[0:64, h:h + 1], qb[0:64, h:h + 1],
                                 op0=ALU.mult, op1=ALU.add)
                IC(ve.tensor_scalar(kha2[h % 2][0:64, c * 512:(c + 1) * 512], pk,
                                    kb[0:64, h:h + 1], None, op0=ALU.add), "V", 1)
            pset("V", regq, regk)
            snap[("qkev", h, c)] = C["V"]

    def emit_Be(h, Q, j):
        qt = 8 * Q + j
        with _FB(f"h{h}e{qt}a") as bk:
            @bk.tensor
            def _(te, h=h, Q=Q, j=j, qt=qt):
                if j == 0 and Q == 0:
                    WT(te, "V", snap[("qkev", h, 3)])
                for kq in range(2):
                    if qt >= 1:
                        WT(te, "V", snap[("nmq", h, qt - 1, kq)])
                    else:
                        pwait(te, PSQR[kq])
                    IC(te.matmul(PSQ[kq],
                                 qha2[h % 2][0:128, qt * 128:(qt + 1) * 128],
                                 kha2[h % 2][0:128, kq * 512:(kq + 1) * 512],
                                 start=True, stop=True, **MM), "T", 1)
                    snap[("eq", h, qt, kq)] = C["T"]
                if j >= 1:
                    WT(te, "V", snap[("scr", h, qt - 1)])
                    IC(te.transpose(nt_ps[0], scrT[:, 0:65], ident[:]), "T", 1)
                    snap[("ntt", h, qt - 1)] = C["T"]

            @bk.vector
            def _(ve, h=h, j=j, qt=qt):
                WT(ve, "T", snap[("eq", h, qt, 0)])
                IC(ve.reduce_max(nm2[:, 0:1], PSQ[0], axis=AX.X), "V", 1)
                snap[("nmq", h, qt, 0)] = C["V"]
                if j >= 1:
                    WT(ve, "T", snap[("ntt", h, qt - 1)])
                    IC(ve.tensor_copy(qha2[h % 2][64:65, (qt - 1) * 128:qt * 128],
                                      nt_ps[0][64:65, 0:128]), "V", 1)
                    snap[("rowev", h, qt - 1)] = C["V"]
                WT(ve, "T", snap[("eq", h, qt, 1)])
                IC(ve.reduce_max(nm2[:, 1:2], PSQ[1], axis=AX.X), "V", 1)
                snap[("nmq", h, qt, 1)] = C["V"]

        with _FB(f"h{h}e{qt}b") as bk:
            @bk.tensor
            def _(te, h=h, qt=qt):
                for kq in range(2, 4):
                    if qt >= 1:
                        WT(te, "V", snap[("nmq", h, qt - 1, kq)])
                    else:
                        pwait(te, PSQR[kq])
                    if kq == 3 and qt >= 1:
                        WT(te, "V", snap[("rowev", h, qt - 1)])
                    IC(te.matmul(PSQ[kq],
                                 qha2[h % 2][0:128, qt * 128:(qt + 1) * 128],
                                 kha2[h % 2][0:128, kq * 512:(kq + 1) * 512],
                                 start=True, stop=True, **MM), "T", 1)
                    snap[("eq", h, qt, kq)] = C["T"]

            @bk.vector
            def _(ve, h=h, qt=qt):
                for kq in range(2, 4):
                    WT(ve, "T", snap[("eq", h, qt, kq)])
                    IC(ve.reduce_max(nm2[:, kq:kq + 1], PSQ[kq], axis=AX.X), "V", 1)
                    snap[("nmq", h, qt, kq)] = C["V"]
                WT(ve, "V", C["V"])
                if qt >= 1:
                    WT(ve, "T", snap[("ntt", h, qt - 1)])
                IC(ve.reduce_max(scrT[:, 64:65], nm2[:, 0:4], axis=AX.X, negate=True), "V", 1)
                snap[("scr", h, qt)] = C["V"]

    def emit_Bend(h, Q):
        qt = 8 * Q + 7
        with _FB(f"h{h}eend{Q}") as bk:
            @bk.tensor
            def _(te, h=h, qt=qt):
                WT(te, "V", snap[("scr", h, qt)])
                IC(te.transpose(nt_ps[0], scrT[:, 0:65], ident[:]), "T", 1)
            snap[("ntt", h, qt)] = C["T"]

            @bk.vector
            def _(ve, h=h, qt=qt):
                WT(ve, "T", snap[("ntt", h, qt)])
                IC(ve.tensor_copy(qha2[h % 2][64:65, qt * 128:(qt + 1) * 128],
                                  nt_ps[0][64:65, 0:128]), "V", 1)
            snap[("rowev", h, qt)] = C["V"]
            snap[("rowdone", h, Q)] = C["V"]
            pset("V", "PS1a", "PS1b", "PS2a", "PS2b")

    def emit_Cct(h, Q, ct):
        p, hi = divmod(h, 2)
        vb = (p % 2) * 2 + hi
        with _FB(f"h{h}s{Q}_{ct}") as bk:
            @bk.tensor
            def _(te, h=h, Q=Q, ct=ct, vb=vb, p=p):
                if ct == 0:
                    WT(te, "V", snap[("rowdone", h, Q)])
                    WT(te, "V", snap[("vrdy", p)])
                    pwait(te, "POa0", "POa1", "POb")
                for v in range(2):
                    if ct >= 1:
                        WT(te, "S", snap[("expq", h, Q, v, ct - 1)])
                        te.matmul(PO[0:65, 1024 + v * 512:1024 + (v + 1) * 512],
                                  vsb4[vb][:, (ct - 1) * 65:ct * 65],
                                  PuT[:, v * 512:(v + 1) * 512],
                                  start=(ct - 1 == 0), stop=False, **MM)
                    mm = te.matmul(PO[:, v * 512:(v + 1) * 512],
                                   kha2[h % 2][0:128, ct * 128:(ct + 1) * 128],
                                   qha2[h % 2][0:128, Q * 1024 + v * 512:Q * 1024 + (v + 1) * 512],
                                   start=True, stop=True, **MM)
                    IC(mm, "T", 1)
                    snap[("etq", h, Q, v, ct)] = C["T"]

            @bk.scalar
            def _(se, h=h, Q=Q, ct=ct):
                WT(se, "T", snap[("etq", h, Q, 1, ct)])
                IC(se.activation(PuT[:, 0:1024], PO[:, 0:1024], AF.Exp), "S", 1)
                snap[("expq", h, Q, 0, ct)] = C["S"]
                snap[("expq", h, Q, 1, ct)] = C["S"]

    def emit_Clast(h, Q):
        p, hi = divmod(h, 2)
        vb = (p % 2) * 2 + hi
        with _FB(f"h{h}sl{Q}") as bk:
            @bk.tensor
            def _(te, h=h, Q=Q, vb=vb):
                ct = NT - 1
                mm = None
                for v in range(2):
                    WT(te, "S", snap[("expq", h, Q, v, ct)])
                    mm = te.matmul(PO[0:65, 1024 + v * 512:1024 + (v + 1) * 512],
                                   vsb4[vb][:, ct * 65:(ct + 1) * 65],
                                   PuT[:, v * 512:(v + 1) * 512],
                                   start=False, stop=True, **MM)
                IC(mm, "T", 1)
            snap[("otdone", h, Q)] = C["T"]
            if Q == 1:
                snap[("etdone", h)] = C["T"]
                snap[("avdone", h)] = C["T"]

    def emit_H4(h, Q):
        p, hi = divmod(h, 2)
        with _FB(f"h{h}n{Q}") as bk:
            @bk.scalar
            def _(se, h=h, Q=Q):
                WT(se, "T", snap[("otdone", h, Q)])
                se.activation(lnr[64:65, 0:1024], PO[64:65, 1024:2048], AF.Ln)
                IC(se.activation(recipr[0:1, 0:1024], lnr[64:65, 0:1024],
                                 AF.Exp, scale=-1.0), "S", 1)
            snap[("rec", h, Q)] = C["S"]

            @bk.tensor
            def _(te, h=h, Q=Q):
                WT(te, "S", snap[("rec", h, Q)])
                mm = None
                for v in range(2):
                    mm = te.matmul(PO[:, v * 512:(v + 1) * 512], ones64[0:1, 0:128],
                                   recipr[0:1, v * 512:(v + 1) * 512], start=True, stop=True, **MM)
                IC(mm, "T", 1)
            snap[("bch", h, Q)] = C["T"]

            @bk.vector
            def _(ve, h=h, Q=Q, hi=hi, p=p):
                WT(ve, "T", snap[("bch", h, Q)])
                ve.tensor_copy(bcast[:, 0:512], PO[:, 0:512])
                IC(ve.tensor_copy(bcast[:, 512:1024], PO[:, 512:1024]), "V", 1)
                WT(ve, "V", C["V"])
                if hi == 0:
                    dst = attn_fm(p)[0:64, Q * 1024:(Q + 1) * 1024]
                else:
                    if h >= 3 and Q == 0:
                        WT(ve, "B", snap[("oddout", h - 2)])
                    dst = tmpodd[0:64, Q * 1024:(Q + 1) * 1024]
                IC(ve.tensor_tensor(dst, PO[0:64, 1024:2048], bcast[0:64, 0:1024],
                                    op=ALU.mult), "V", 1)
            snap[("mul", h, Q)] = C["V"]
            if Q == 1:
                snap[("mul", h)] = C["V"]
            pset("V", "POa0", "POa1", "POb")

        if hi == 1 and Q == 1:
            with _FB(f"h{h}n3") as bk:
                @bk.sync
                def _(sy, h=h, p=p):
                    WT(sy, "V", snap[("mul", h)])
                    IC(sy.dma_start(out=oddb, in_=tmpodd[0:64, :]), "B", 16)
                    snap[("oddout", h)] = C["B"]
                    WT(sy, "B", C["B"])
                    IC(sy.dma_start(out=attn_fm(p)[64:128, :], in_=oddb), "B", 16)
                snap[("oddin", h)] = C["B"]

    def lane_items_A_B0(h):
        items = []
        if h % 2 == 0:
            items += [("Av", h, tt) for tt in range(NT)]
        items += [("Aqk", h, c) for c in range(4)]
        items += [("Be", h, 0, j) for j in range(8)] + [("Bend", h, 0)]
        return items

    def lane_items_B1(h):
        p, hi = divmod(h, 2)
        items = [("Be", h, 1, j) for j in range(8)] + [("Bend", h, 1)]
        if hi == 1:
            if p + 1 < NH // 2:
                items += [("QT", p + 1, r) for r in range(3)]
            if p + 2 < NH // 2:
                items += [("QC", p + 2)]
            if p == NH // 2 - 2:
                items += [("FCC", et) for et in range(3)]
        return items

    def lane_items_C(h, Q):
        return ([("Cct", h, Q, ct) for ct in range(NT)] +
                [("Clast", h, Q), ("H4", h, Q)])

    def emit_item(it):
        kind = it[0]
        if kind == "Av":
            emit_Av(it[1], it[2])
        elif kind == "Aqk":
            emit_Aqk(it[1], it[2])
        elif kind == "Be":
            emit_Be(it[1], it[2], it[3])
        elif kind == "Bend":
            emit_Bend(it[1], it[2])
        elif kind == "Cct":
            emit_Cct(it[1], it[2], it[3])
        elif kind == "Clast":
            emit_Clast(it[1], it[2])
        elif kind == "H4":
            emit_H4(it[1], it[2])
        elif kind == "QT":
            emit_qkv_transpose_r(it[1], it[2])
        elif kind == "QC":
            emit_qkv_chunks(it[1])
        elif kind == "FCC":
            emit_fc_chunk(it[1])

    def emit_interleaved(l1, l2):
        n1, n2 = len(l1), len(l2)
        i1 = i2 = 0
        while i1 < n1 or i2 < n2:
            if i2 >= n2 or (i1 < n1 and i1 * max(n2, 1) <= i2 * max(n1, 1)):
                emit_item(l1[i1])
                i1 += 1
            else:
                emit_item(l2[i2])
                i2 += 1

    prev_tail = []
    for h in range(NH):
        emit_interleaved(prev_tail, lane_items_A_B0(h))
        emit_interleaved(lane_items_C(h, 0), lane_items_B1(h))
        prev_tail = lane_items_C(h, 1)
        if _DEBUG and h == 0:
            with _FB("dbg_qk") as bk:
                @bk.sync
                def _(sy):
                    WT(sy, "V", snap[("rowdone", 0, 1)])
                    U16 = mybir.dt.uint16
                    IC(sy.dma_start(out=dbg["const"][:, 0:16], in_=hm_b[:, :]), "B", 16)
                    IC(sy.dma_start(out=dbg["const"][:, 16:32], in_=hm2[:, :]), "B", 16)
                    IC(sy.dma_start(out=dbg["const"][:, 32:48], in_=qb[:, :]), "B", 16)
                    IC(sy.dma_start(out=dbg["const"][:, 48:64], in_=kb[:, :]), "B", 16)
                    IC(sy.dma_start(out=dbg["qha"], in_=qha2[0][:, :].bitcast(U16)), "B", 16)
                    IC(sy.dma_start(out=dbg["kha"], in_=kha2[0][:, :].bitcast(U16)), "B", 16)
                    IC(sy.dma_start(out=dbg["vsb"], in_=vsb4[0][:, :].bitcast(U16)), "B", 16)
            snap["dbgqk"] = C["B"]
    for it in prev_tail:
        emit_item(it)
    if _DEBUG:
        with _FB("dbg_afm") as bk:
            @bk.sync
            def _(sy):
                WT(sy, "V", snap[("mul", 0)])
                WT(sy, "B", snap[("oddin", 1)])
                IC(sy.dma_start(out=dbg["afm"], in_=attn_fm(0)[:, :].bitcast(mybir.dt.uint16)), "B", 16)


    fcbf = [xtok[0], xtok[1]]
    PS1bf_full = PS1[:, 0:512].bitcast(BF16)
    for et in range(8):
        if et >= 3:
            emit_fc_chunk(et)
        with _FB(f"fcp{et}") as bk:
            @bk.gpsimd
            def _(pl, et=et):
                ci = snap["fcch"][et]
                rwait(pl, ci)
                if et >= 2:
                    WT(pl, "T", snap[("fctr", et - 2)])
                IC(pl.tensor_copy(fcbf[et % 2][:, 0:1024], slotview(ci, 128, 1024)), "K", 1)
                set_rel(ci, ["K"])
            snap[("fcc", et)] = C["K"]

        with _FB(f"fct{et}") as bk:
            @bk.tensor
            def _(te, et=et):
                WT(te, "K", snap[("fcc", et)])
                pwait(te, "PS1a")
                mm = None
                for h16 in range(16):
                    mm = te.transpose(PS1bf_full[64 * (h16 % 2):64 * (h16 % 2) + 64,
                                                 128 * (h16 // 2):128 * (h16 // 2) + 128],
                                      fcbf[et % 2][:, h16 * 64:(h16 + 1) * 64],
                                      ident[:], tile_position=(0, 64 * (h16 % 2)))
                IC(mm, "T", 1)
            snap[("fctr", et)] = C["T"]

            @bk.vector
            def _(ve, et=et):
                WT(ve, "T", snap[("fctr", et)])
                last = None
                for half in range(2):
                    src = PS1bf_full[64 * half:64 * half + 64, :].rearrange("p (b c) -> p b c", c=128)
                    dst = fcT[64 * half:64 * half + 64, :].rearrange("p (b c) -> p b c", c=1024)[:, :, et * 128:(et + 1) * 128]
                    last = ve.tensor_copy(dst, src)
                IC(last, "V", 1)
            pset("V", "PS1a")
    snap["fcT_rdy"] = C["V"]

    for qt in range(NT):
        with _FB(f"fc{qt}") as bk:
            @bk.tensor
            def _(te, qt=qt):
                if qt == 0:
                    WT(te, "V", snap["fcT_rdy"])
                    for hh in range(0, NH, 2):
                        WT(te, "V", snap[("mul", hh)])
                    for hh in range(1, NH, 2):
                        WT(te, "B", snap[("oddin", hh)])
                regsE = ("PS1a", "PS1b") if qt % 2 == 0 else ("POa0", "POa1")
                pwait(te, *regsE)
                dstE = PS1[:, :] if qt % 2 == 0 else PO[:, 0:1024]
                for pp in range(8):
                    for dh in range(2):
                        te.matmul(dstE[:, dh * 512:(dh + 1) * 512],
                                  attn_fm(pp)[0:128, qt * 128:(qt + 1) * 128],
                                  fcT[0:128, pp * 1024 + dh * 512:pp * 1024 + (dh + 1) * 512],
                                  start=(pp == 0), stop=False, **MM)
                mm = None
                for dh in range(2):
                    mm = te.matmul(dstE[:, dh * 512:(dh + 1) * 512], ones0[0:1, 0:128],
                                   fcb_row[0:1, dh * 512:(dh + 1) * 512], start=False, stop=True, **MM)
                IC(mm, "T", 1)
            snap[("fcmm", qt)] = C["T"]

            @bk.vector
            def _(ve, qt=qt):
                WT(ve, "T", snap[("fcmm", qt)])
                dstE = PS1[:, :] if qt % 2 == 0 else PO[:, 0:1024]
                xt = x16[:, qt * 1024:(qt + 1) * 1024]
                IC(ve.tensor_tensor(xt, dstE, xt, op=ALU.add), "V", 1)
            if qt % 2 == 0:
                pset("V", "PS1a", "PS1b")
            else:
                pset("V", "POa0", "POa1")
            snap[("fcev", qt)] = C["V"]
    snap["fc_done_t"] = C["T"]
    snap["fc_done_v"] = C["V"]

    if _DEBUG:
        with _FB("dbg_x16a") as bk:
            @bk.sync
            def _(sy):
                WT(sy, "V", C["V"])
                IC(sy.dma_start(out=dbg["x16a"], in_=x16[:, :].bitcast(mybir.dt.uint16)), "B", 16)

    emit_layernorm("ln2", with_logits=True)
    snap["ln2_done"] = C["V"]

    with _FB("gat1") as bk:
        @bk.vector
        def _(ve):
            last = None
            for t in range(NT):
                WT(ve, "V", snap[("ln2lev", t)])
                last = ve.reduce_max(maxs1[:, t:t + 1], logits[:, t * 8:(t + 1) * 8], axis=AX.X)
            IC(last, "V", 1)
            WT(ve, "V", C["V"])
            for t in range(NT):
                ve.tensor_scalar(masks1[:, t * 8:(t + 1) * 8], logits[:, t * 8:(t + 1) * 8],
                                 maxs1[:, t:t + 1], None, op0=ALU.is_ge)
            for t in range(NT):
                ve.scalar_tensor_tensor(gt2[:, (t % 2) * 8:(t % 2) * 8 + 8],
                                        masks1[:, t * 8:(t + 1) * 8], -1e30,
                                        logits[:, t * 8:(t + 1) * 8], op0=ALU.mult, op1=ALU.add)
                if t >= 1:
                    ve.reduce_max(maxs2[:, t - 1:t], gt2[:, ((t - 1) % 2) * 8:((t - 1) % 2) * 8 + 8], axis=AX.X)
            IC(ve.reduce_max(maxs2[:, NT - 1:NT], gt2[:, ((NT - 1) % 2) * 8:((NT - 1) % 2) * 8 + 8], axis=AX.X), "V", 1)
            WT(ve, "V", C["V"])
            for t in range(NT):
                ve.scalar_tensor_tensor(gt2[:, (t % 2) * 8:(t % 2) * 8 + 8],
                                        masks1[:, t * 8:(t + 1) * 8], -1e30,
                                        logits[:, t * 8:(t + 1) * 8], op0=ALU.mult, op1=ALU.add)
                if t >= 1:
                    ve.tensor_scalar(masks2[:, (t - 1) * 8:t * 8], gt2[:, ((t - 1) % 2) * 8:((t - 1) % 2) * 8 + 8],
                                     maxs2[:, t - 1:t], None, op0=ALU.is_ge)
            ve.tensor_scalar(masks2[:, (NT - 1) * 8:NT * 8], gt2[:, ((NT - 1) % 2) * 8:((NT - 1) % 2) * 8 + 8],
                             maxs2[:, NT - 1:NT], None, op0=ALU.is_ge)
            IC(ve.tensor_tensor(tmp16[:], maxs2[:], maxs1[:], op=ALU.subtract), "V", 1)
    snap["gd"] = C["V"]

    with _FB("gat2") as bk:
        @bk.scalar
        def _(se):
            WT(se, "V", snap["gd"])
            IC(se.activation(e2t[:], tmp16[:], AF.Exp), "S", 1)
    snap["ge"] = C["S"]

    with _FB("gat3") as bk:
        @bk.vector
        def _(ve):
            WT(ve, "S", snap["ge"])
            IC(ve.tensor_scalar(tmp16[:], e2t[:], 1.0, None, op0=ALU.add), "V", 1)
            WT(ve, "V", C["V"])
            IC(ve.reciprocal(rden[:], tmp16[:]), "V", 1)
            WT(ve, "V", C["V"])
            IC(ve.tensor_tensor(w2g[:], e2t[:], rden[:], op=ALU.mult), "V", 1)
            WT(ve, "V", C["V"])
            for t in range(NT):
                ve.tensor_scalar(gates[:, t * 8:(t + 1) * 8], masks1[:, t * 8:(t + 1) * 8],
                                 rden[:, t:t + 1], None, op0=ALU.mult)
            last = None
            for t in range(NT):
                last = ve.scalar_tensor_tensor(gates[:, t * 8:(t + 1) * 8], masks2[:, t * 8:(t + 1) * 8],
                                               w2g[:, t:t + 1], gates[:, t * 8:(t + 1) * 8],
                                               op0=ALU.mult, op1=ALU.add)
            IC(last, "V", 1)
    snap["gates_done"] = C["V"]

    if _DEBUG:
        with _FB("dbg_g") as bk:
            @bk.sync
            def _(sy):
                WT(sy, "V", snap["gates_done"])
                IC(sy.dma_start(out=dbg["gates"], in_=gates[:, :]), "B", 16)

    def emit_moe_chunks(egi):
        e, g = divmod(egi, 4)
        par = egi % 2
        chunks = []
        if egi == 0:
            w1_free = [("V", snap["fc_done_v"])]
            w2_free = [("V", snap["fc_done_v"])]
        elif egi == 1:
            w1_free = [("T", snap[("ln2tr", NT - 1)]), ("S", snap["ln2sq"]),
                       ("V", snap[("ln2nm", NT - 1)])]
            w2_free = [("V", snap["fc_done_v"])]
        else:
            w1_free = [("T", snap[("hdone", egi - 2)])]
            w2_free = [("T", snap[("odone", egi - 2)])]
        for kt in range(NKT):
            chunks.append((w1[e, kt * 128:(kt + 1) * 128, g * 1024:(g + 1) * 1024], 128, 1024,
                           w1v[par][:, kt * 1024:(kt + 1) * 1024],
                           w1_free if kt == 0 else None))
        for itl in range(8):
            chunks.append((w2[e, (g * 8 + itl) * 128:(g * 8 + itl + 1) * 128, :], 128, 1024,
                           w2v[par][:, itl * 1024:(itl + 1) * 1024],
                           w2_free if itl == 0 else None))
        if g == 1:
            chunks.append((b2[e:e + 1, :], 1, 1024, b2_row[:],
                           [("T", snap[("odone", egi - 2)])] if egi >= 2 else [("V", snap["fc_done_v"])]))
        for i, (src, pp, nn, dst, dwait) in enumerate(chunks):
            with _FB(f"mc{egi}_{i}") as bk:
                hold = []

                @bk.sync
                def _(sy, src=src, pp=pp, nn=nn):
                    hold.append(ring_dma(sy, src, pp, nn))

                @bk.gpsimd
                def _(pl, pp=pp, nn=nn, dst=dst, dwait=dwait):
                    ci = hold[0]
                    rwait(pl, ci)
                    if dwait is not None:
                        for (ds, dv) in dwait:
                            WT(pl, ds, dv)
                    IC(pl.tensor_copy(dst, slotview(ci, pp, nn)), "K", 1)
                    set_rel(ci, ["K"])
        snap[("wk", egi)] = C["K"]

    emit_moe_chunks(0)

    for egi in range(NE * 4):
        e, g = divmod(egi, 4)
        par = egi % 2
        if egi + 1 < NE * 4:
            emit_moe_chunks(egi + 1)
        for itl in range(8):
            with _FB(f"m{egi}h{itl}") as bk:
                @bk.tensor
                def _(te, egi=egi, e=e, g=g, par=par, itl=itl):
                    if itl == 0:
                        WT(te, "K", snap[("wk", egi)])
                        if egi == 0:
                            WT(te, "V", snap[("ln2ev", NT - 1)])
                    for tq in range(4):
                        if itl == 0:
                            pwait(te, PSQR[tq])
                        else:
                            WT(te, "S", snap[("gel", egi, itl - 1, tq)])
                        mm = None
                        for kt in range(NKT):
                            mm = te.matmul(PSQ[tq], w1v[par][:, kt * 1024 + itl * 128:kt * 1024 + (itl + 1) * 128],
                                           xnfm[:, kt * 2048 + tq * 512:kt * 2048 + (tq + 1) * 512],
                                           start=(kt == 0), stop=(kt == NKT - 1), **MM)
                        IC(mm, "T", 1)
                        snap[("hmm", egi, itl, tq)] = C["T"]

                @bk.scalar
                def _(se, egi=egi, e=e, g=g, itl=itl):
                    if egi == 0 and itl == 0:
                        WT(se, "T", snap["fc_done_t"])
                    for tq in range(4):
                        WT(se, "T", snap[("hmm", egi, itl, tq)])
                        IC(se.activation(h_g[:, itl * 2048 + tq * 512:itl * 2048 + (tq + 1) * 512],
                                         PSQ[tq], AF.Gelu,
                                         bias=b1fm[:, e * 32 + g * 8 + itl:e * 32 + g * 8 + itl + 1]), "S", 1)
                        snap[("gel", egi, itl, tq)] = C["S"]
                        pset("S", PSQR[tq])
        snap[("hdone", egi)] = C["T"]

        for tc in range(NT):
            with _FB(f"m{egi}o{tc}") as bk:
                @bk.tensor
                def _(te, egi=egi, e=e, g=g, par=par, tc=tc):
                    if tc == 0:
                        for tq in range(4):
                            WT(te, "S", snap[("gel", egi, 7, tq)])
                    if tc % 2 == 0:
                        pwait(te, "POa0", "POa1")
                    else:
                        pwait(te, "POb")
                    base = (tc % 2) * 1024
                    mm = None
                    for itl in range(8):
                        for dh in range(2):
                            last = (itl == 7 and g != 3)
                            mm = te.matmul(PO[:, base + dh * 512:base + (dh + 1) * 512],
                                           h_g[:, itl * 2048 + tc * 128:itl * 2048 + (tc + 1) * 128],
                                           w2v[par][:, itl * 1024 + dh * 512:itl * 1024 + (dh + 1) * 512],
                                           start=(itl == 0), stop=last, **MM)
                    if g == 3:
                        for dh in range(2):
                            mm = te.matmul(PO[:, base + dh * 512:base + (dh + 1) * 512],
                                           ones0[0:1, 0:128], b2_row[0:1, dh * 512:(dh + 1) * 512],
                                           start=False, stop=True, **MM)
                    IC(mm, "T", 1)
                snap[("omm", egi, tc)] = C["T"]

                @bk.vector
                def _(ve, egi=egi, e=e, tc=tc):
                    WT(ve, "T", snap[("omm", egi, tc)])
                    if egi == 0 and tc == 0:
                        WT(ve, "V", snap["gates_done"])
                    base = (tc % 2) * 1024
                    xt = x16[:, tc * 1024:(tc + 1) * 1024]
                    IC(ve.scalar_tensor_tensor(xt, PO[:, base:base + 1024],
                                               gates[:, tc * 8 + e:tc * 8 + e + 1], xt,
                                               op0=ALU.mult, op1=ALU.add), "V", 1)
                pset("V", *(("POa0", "POa1") if tc % 2 == 0 else ("POb",)))
                snap[("oev", egi, tc)] = C["V"]
        snap[("odone", egi)] = C["T"]

    last = NE * 4 - 1
    for t in range(NT):
        with _FB(f"out{t}") as bk:
            @bk.vector
            def _(ve, t=t):
                WT(ve, "V", snap[("oev", last, t)])
                if t == 0:
                    WT(ve, "K", C["K"])
                if t >= 3:
                    WT(ve, "Q%d" % (t % 3), 16 * (t // 3))
                sl = t % 3
                IC(ve.tensor_copy(stage[:, sl * 1024:(sl + 1) * 1024], x16[:, t * 1024:(t + 1) * 1024]), "V", 1)
            snap[("outc", t)] = C["V"]

            @bk.sync
            def _(sy, t=t):
                WT(sy, "V", snap[("outc", t)])
                sl = t % 3
                IC(sy.dma_start(out=xout[t * 128:(t + 1) * 128, :], in_=stage[:, sl * 1024:(sl + 1) * 1024]), "Q%d" % (t % 3), 16)

    with _FB("fin") as bk:
        @bk.sync
        def _(sy):
            WT(sy, "Q0", C["Q0"])
            WT(sy, "Q1", C["Q1"])
            WT(sy, "Q2", C["Q2"])
            WT(sy, "B", C["B"])

    assert 16384 + (212863 - nc.sbuf_bytes_remaining) <= A0 - 512, nc.sbuf_bytes_remaining
    stack.close()
    return nc


_NC_CACHE = None


def kernel(**inputs):
    global _NC_CACHE
    if _NC_CACHE is None:
        _NC_CACHE = build_nc()
    nc = _NC_CACHE
    x = np.ascontiguousarray(inputs["L0_attn_x"], dtype=np.float32)
    common = {
        "hm": np.ascontiguousarray(inputs["L0_attn_head_mask"], dtype=np.float32),
        "qkvw": np.ascontiguousarray(inputs["L0_attn_qkv_w"], dtype=np.float32),
        "qkvb": np.ascontiguousarray(inputs["L0_attn_qkv_b"], dtype=np.float32),
        "fcw": np.ascontiguousarray(inputs["L0_attn_fc_w"], dtype=np.float32),
        "fcb": np.ascontiguousarray(inputs["L0_attn_fc_b"], dtype=np.float32),
        "wg": np.ascontiguousarray(inputs["L0_ffn_Wg"], dtype=np.float32),
        "w1": np.ascontiguousarray(inputs["L0_ffn_W1"], dtype=np.float32),
        "b1": np.ascontiguousarray(inputs["L0_ffn_b1"], dtype=np.float32),
        "w2": np.ascontiguousarray(inputs["L0_ffn_W2"], dtype=np.float32),
        "b2": np.ascontiguousarray(inputs["L0_ffn_b2"], dtype=np.float32),
    }
    in_maps = [dict(common, xin=np.ascontiguousarray(x[c])) for c in range(NCORES)]
    res = run_bass_kernel_spmd(nc, in_maps, core_ids=list(range(NCORES)))
    out = np.stack([np.asarray(res.results[c]["xout"], dtype=np.float32) for c in range(NCORES)], axis=0)
    return out


# version 22 (final)
# speedup vs baseline: 1.1038x; 1.1038x over previous
import numpy as np
import concourse.bass as bass
import concourse.mybir as mybir
from concourse.bass_utils import run_bass_kernel_spmd

F32 = mybir.dt.float32
BF16 = mybir.dt.bfloat16
AF = mybir.ActivationFunctionType
ALU = mybir.AluOpType
AX = mybir.AxisListType

NCORES = 8
S, D, NH, HD = 2048, 1024, 16, 64
NT, NKT = 16, 8
NE, INTER = 8, 4096
EPS = 1e-5
_DEBUG = False


def build_nc():
    nc = bass.Bass("TRN2", target_bir_lowering=False)

    xin = nc.dram_tensor("xin", [S, D], F32, kind="ExternalInput").ap()
    hm_d = nc.dram_tensor("hm", [NH], F32, kind="ExternalInput").ap()
    qkvw = nc.dram_tensor("qkvw", [3 * D, D], F32, kind="ExternalInput").ap()
    qkvb = nc.dram_tensor("qkvb", [3 * D], F32, kind="ExternalInput").ap()
    fcw = nc.dram_tensor("fcw", [D, D], F32, kind="ExternalInput").ap()
    fcb = nc.dram_tensor("fcb", [D], F32, kind="ExternalInput").ap()
    wg = nc.dram_tensor("wg", [NE, D], F32, kind="ExternalInput").ap()
    w1 = nc.dram_tensor("w1", [NE, D, INTER], F32, kind="ExternalInput").ap()
    b1 = nc.dram_tensor("b1", [NE, INTER], F32, kind="ExternalInput").ap()
    w2 = nc.dram_tensor("w2", [NE, INTER, D], F32, kind="ExternalInput").ap()
    b2 = nc.dram_tensor("b2", [NE, D], F32, kind="ExternalInput").ap()
    xout = nc.dram_tensor("xout", [S, D], F32, kind="ExternalOutput").ap()
    oddb = nc.dram_tensor("oddb", [64, 2048], BF16, kind="Internal").ap()
    dbg = {}
    if _DEBUG:
        U16 = mybir.dt.uint16
        dbg["x16a"] = nc.dram_tensor("dbg_x16a", [128, NT * 1024], U16, kind="ExternalOutput").ap()
        dbg["gates"] = nc.dram_tensor("dbg_gates", [128, 128], F32, kind="ExternalOutput").ap()
        dbg["xnfm1"] = nc.dram_tensor("dbg_xnfm1", [128, NKT * 2048], U16, kind="ExternalOutput").ap()
        dbg["qha"] = nc.dram_tensor("dbg_qha", [128, 2048], U16, kind="ExternalOutput").ap()
        dbg["kha"] = nc.dram_tensor("dbg_kha", [128, 2048], U16, kind="ExternalOutput").ap()
        dbg["vsb"] = nc.dram_tensor("dbg_vsb", [128, 1040], U16, kind="ExternalOutput").ap()
        dbg["afm"] = nc.dram_tensor("dbg_afm", [128, 2048], U16, kind="ExternalOutput").ap()
        dbg["const"] = nc.dram_tensor("dbg_const", [128, 64], F32, kind="ExternalOutput").ap()

    x16 = nc.alloc_sbuf_tensor("x16", [128, NT * 1024], BF16)
    xnfm = nc.alloc_sbuf_tensor("xnfm", [128, NKT * 2048], BF16)
    stage = nc.alloc_sbuf_tensor("stage", [128, 3 * 1024], F32)
    TOP = 229184
    A0, B0, C0 = TOP - 98304, TOP - 65536, TOP - 32768

    def _at(name, shape, dtype, off):
        return nc.alloc_sbuf_tensor_at(name, shape, dtype, offset=off)
    h_g = _at("h_g", [128, 16384], BF16, A0)
    afm_t = [_at("afm%d" % p, [128, 2048], BF16, A0 + 4096 * p) for p in range(8)]
    w1v_t = [_at("w1v0", [128, 8192], BF16, B0), _at("w1v1", [128, 8192], BF16, B0 + 16384)]
    qkvT_t = [_at("qkvT0", [128, 3072], BF16, B0), _at("qkvT1", [128, 3072], BF16, B0 + 6144)]
    vsb_t = [_at("vsb0", [128, 1040], BF16, B0 + 12288), _at("vsb1", [128, 1040], BF16, B0 + 14400)]
    PuT_t = _at("PuTt", [128, 2048], BF16, B0 + 16512)
    bcast_t = _at("bcastt", [128, 2048], BF16, B0 + 20608)
    xtok_t = [_at("xtok0", [128, 1024], BF16, B0 + 24704), _at("xtok1", [128, 1024], BF16, B0 + 26752)]
    sqjunk_t = _at("sqjunkt", [128, 1024], BF16, B0 + 28800)
    w2v_t = [_at("w2v0", [128, 8192], BF16, C0), _at("w2v1", [128, 8192], BF16, C0 + 16384)]
    fcT_t = _at("fcTt", [128, 8192], BF16, C0)
    qha_t = _at("qhat", [128, 2048], BF16, C0 + 16384)
    kha_t = _at("khat", [128, 2048], BF16, C0 + 20480)
    tmpodd_t = _at("tmpoddt", [128, 2048], BF16, C0 + 24576)
    lnr = _at("lnr", [65, 1024], F32, C0 + 28672)
    qha_b = nc.alloc_sbuf_tensor("qha_b", [128, 2048], BF16)
    kha_b = nc.alloc_sbuf_tensor("kha_b", [128, 2048], BF16)
    vsb2 = nc.alloc_sbuf_tensor("vsb2", [128, 1040], BF16)
    vsb3 = nc.alloc_sbuf_tensor("vsb3", [128, 1040], BF16)
    rowp = nc.alloc_sbuf_tensor("rowp", [128, 3712], BF16)
    ident = nc.alloc_sbuf_tensor("ident", [128, 128], BF16)
    identF = nc.alloc_sbuf_tensor("identF", [128, 128], F32)
    scrT = nc.alloc_sbuf_tensor("scrT", [128, 128], BF16)
    masks1 = nc.alloc_sbuf_tensor("masks1", [128, 128], BF16)
    masks2 = nc.alloc_sbuf_tensor("masks2", [128, 128], BF16)
    logits = nc.alloc_sbuf_tensor("logits", [128, 128], F32)
    gates = nc.alloc_sbuf_tensor("gates", [128, 128], F32)
    gt2 = nc.alloc_sbuf_tensor("gt2", [128, 16], F32)
    hm_b = nc.alloc_sbuf_tensor("hm_b", [128, 16], F32)
    hm2 = nc.alloc_sbuf_tensor("hm2", [128, 16], F32)
    qb = nc.alloc_sbuf_tensor("qb", [128, 16], F32)
    kb = nc.alloc_sbuf_tensor("kb", [128, 16], F32)
    b1fm = nc.alloc_sbuf_tensor("b1fm", [128, 256], F32)
    WgTb = nc.alloc_sbuf_tensor("WgTb", [128, 64], BF16)
    sums = nc.alloc_sbuf_tensor("sums", [128, 16], F32)
    sumsq = nc.alloc_sbuf_tensor("sumsq", [128, 16], F32)
    meanv = nc.alloc_sbuf_tensor("meanv", [128, 16], F32)
    varv = nc.alloc_sbuf_tensor("varv", [128, 16], F32)
    tmp16 = nc.alloc_sbuf_tensor("tmp16", [128, 16], F32)
    rstd = nc.alloc_sbuf_tensor("rstd", [128, 16], F32)
    negmr = nc.alloc_sbuf_tensor("negmr", [128, 16], F32)
    nm2 = nc.alloc_sbuf_tensor("nm2", [128, 4], F32)
    maxs1 = nc.alloc_sbuf_tensor("maxs1", [128, 16], F32)
    maxs2 = nc.alloc_sbuf_tensor("maxs2", [128, 16], F32)
    e2t = nc.alloc_sbuf_tensor("e2t", [128, 16], F32)
    rden = nc.alloc_sbuf_tensor("rden", [128, 16], F32)
    w2g = nc.alloc_sbuf_tensor("w2g", [128, 16], F32)
    hmrow = nc.alloc_sbuf_tensor("hmrow", [1, 16], F32)

    def attn_fm(p):
        return afm_t[p][:, :]
    w1v = [w1v_t[0][:, :], w1v_t[1][:, :]]
    qkvT = [qkvT_t[0][:, :], qkvT_t[1][:, :]]
    v_sb = [vsb_t[0][:, :], vsb_t[1][:, :]]
    PuT = PuT_t[:, :]
    bcast = bcast_t[:, :]
    xtok = [xtok_t[0][:, :], xtok_t[1][:, :]]
    sqjunk = sqjunk_t[:, :]
    w2v = [w2v_t[0][:, :], w2v_t[1][:, :]]
    fcT = fcT_t[:, :]
    qha = qha_t[:, :]
    kha = kha_t[:, :]
    tmpodd = tmpodd_t[:, :]
    qha2 = [qha_t[:, :], qha_b[:, :]]
    kha2 = [kha_t[:, :], kha_b[:, :]]
    vsb4 = [vsb_t[0][:, :], vsb_t[1][:, :], vsb2[:, :], vsb3[:, :]]
    vb_row = rowp[0:1, 0:1024]
    fcb_row = rowp[0:1, 1024:2048]
    b2_row = rowp[0:1, 2048:3072]
    ones0 = rowp[0:1, 3072:3584]
    hm_bf = rowp[0:1, 3584:3600]
    ones64 = rowp[64:65, 0:512]
    recipr = rowp[64:65, 512:2560]

    PS1 = nc.alloc_psum_tensor("PS1", [128, 1024], F32)
    PS2 = nc.alloc_psum_tensor("PS2", [128, 1024], F32)
    PO = nc.alloc_psum_tensor("PO", [128, 2048], F32)
    pbfLN = [PS1[:, 0:512].bitcast(BF16), PS2[:, 0:512].bitcast(BF16)]
    nt_ps = [PS2[0:65, 960:1024].bitcast(BF16), PO[0:65, 1984:2048].bitcast(BF16)]

    import contextlib
    stack = contextlib.ExitStack()
    sems = {}
    for nm, k in zip(["sK", "sD", "sP", "sT", "sS", "sV", "sB",
                      "sR0", "sR1", "sR2", "sQ0", "sQ1", "sQ2"],
                     ["K", "D", "P", "T", "S", "V", "B",
                      "R0", "R1", "R2", "Q0", "Q1", "Q2"]):
        sems[k] = stack.enter_context(nc.semaphore(nm))
    C = {k: 0 for k in ["K", "D", "P", "T", "S", "V", "B",
                        "R0", "R1", "R2", "Q0", "Q1", "Q2"]}
    snap = {}
    pfree = {r: ("T", 0) for r in ["PS1a", "PS1b", "PS2a", "PS2b", "POa0", "POa1", "POb"]}
    PSQ = [PS1[:, 0:512], PS1[:, 512:1024], PS2[:, 0:512], PS2[:, 512:1024]]
    PSQR = ["PS1a", "PS1b", "PS2a", "PS2b"]

    def IC(instr, s, n=1):
        instr.then_inc(sems[s], n)
        C[s] += n

    def WT(eng, s, v):
        if v > 0:
            eng.wait_ge(sems[s], v)

    def pwait(eng, *regs):
        for r in regs:
            s, v = pfree[r]
            WT(eng, s, v)

    def pset(s, *regs):
        for r in regs:
            pfree[r] = (s, C[s])

    RING = {"ci": 0, "rel": [], "cw": []}

    def slotview(ci, p, n):
        sl = ci % 3
        return stage[0:p, sl * 1024: sl * 1024 + n]

    def ring_dma(sy, src, p, n):
        ci = RING["ci"]
        RING["ci"] += 1
        for (s, v) in RING["rel"][ci - 3] if ci >= 3 else seed_rel:
            WT(sy, s, v)
        key = "R%d" % (ci % 3)
        IC(sy.dma_start(out=slotview(ci, p, n), in_=src), key, 16)
        RING["rel"].append(None)
        RING["cw"].append((key, C[key]))
        return ci

    def rwait(eng, ci):
        key, v = RING["cw"][ci]
        WT(eng, key, v)

    def set_rel(ci, slist):
        RING["rel"][ci] = [(s, C[s]) for s in slist]


    class _FB:
        def __init__(self, name=None):
            pass

        def __enter__(self):
            return self

        def __exit__(self, *a):
            return False

        def sync(self, f):
            f(nc.sync)

        def tensor(self, f):
            f(nc.tensor)

        def vector(self, f):
            f(nc.vector)

        def scalar(self, f):
            f(nc.scalar)

        def gpsimd(self, f):
            f(nc.gpsimd)

    MM = dict(skip_group_check=True)

    with _FB("su_d1") as bk:
        @bk.sync
        def _(sy):
            IC(sy.dma_start(out=stage[0:1, 0:3072], in_=qkvb.rearrange("(a b) -> a b", a=1)), "D", 16)
            IC(sy.dma_start(out=hmrow[0:1, :], in_=hm_d.rearrange("(a b) -> a b", a=1)), "D", 16)
            for h in range(NH):
                IC(sy.dma_start(out=qb[0:64, h:h + 1],
                                in_=qkvb[192 * h:192 * h + 64].rearrange("(p o) -> p o", o=1)), "D", 16)
                IC(sy.dma_start(out=kb[0:64, h:h + 1],
                                in_=qkvb[192 * h + 64:192 * h + 128].rearrange("(p o) -> p o", o=1)), "D", 16)
    snap["d1"] = C["D"]

    with _FB("su_p1") as bk:
        @bk.gpsimd
        def _(pl):
            IC(pl.memset(ident[:], 0.0), "P", 1)
            WT(pl, "P", C["P"])
            pl.affine_select(out=ident[:], in_=ident[:], compare_op=ALU.not_equal,
                             fill=1.0, base=0, pattern=[[-1, 128]], channel_multiplier=1)
            IC(pl.memset(identF[:], 0.0), "P", 1)
            WT(pl, "P", C["P"])
            pl.affine_select(out=identF[:], in_=identF[:], compare_op=ALU.not_equal,
                             fill=1.0, base=0, pattern=[[-1, 128]], channel_multiplier=1)
            pl.memset(scrT[:], 0.0)
            pl.memset(ones0[:], 1.0)
            pl.memset(ones64[:], 1.0)
            pl.memset(qha_t[64:128, :], 0.0)
            pl.memset(qha_b[64:128, :], 0.0)
            IC(pl.memset(kha_t[64:128, :], 0.0), "P", 1)
            WT(pl, "P", C["P"])
            IC(pl.memset(kha_b[64:128, :], 0.0), "P", 1)
            WT(pl, "P", C["P"])
            pl.memset(kha[64:65, :], 1.0)
            pl.memset(kha_b[64:65, :], 1.0)
            for vv in (v_sb[0], v_sb[1], vsb2[:, :], vsb3[:, :]):
                pl.memset(vv.rearrange("p (blk c) -> p blk c", c=65)[:, :, 64:65], 1.0)
            WT(pl, "D", snap["d1"])
            src = stage[0:1, 0:3072].rearrange("p (h c) -> p h c", c=192)[:, :, 128:192]
            pl.tensor_copy(vb_row.rearrange("p (h c) -> p h c", c=64), src)
            IC(pl.tensor_copy(hm_bf[:], hmrow[:]), "P", 1)
    snap["p1"] = C["P"]

    with _FB("su_d2") as bk:
        @bk.sync
        def _(sy):
            WT(sy, "P", snap["p1"])
            IC(sy.dma_start(out=stage[0:1, 0:1024], in_=fcb.rearrange("(a b) -> a b", a=1)), "D", 16)
            for e in range(NE):
                IC(sy.dma_start(out=stage[0:32, 1024 + e * 128:1024 + (e + 1) * 128],
                                in_=b1[e, :].rearrange("(a b) -> a b", b=128)), "D", 16)
            IC(sy.dma_start(out=stage[0:8, 2048:3072], in_=wg[:, :]), "D", 16)
    snap["d2"] = C["D"]

    with _FB("su_p2") as bk:
        @bk.gpsimd
        def _(pl):
            WT(pl, "D", snap["d2"])
            IC(pl.tensor_copy(fcb_row[:], stage[0:1, 0:1024]), "P", 1)
    snap["p2"] = C["P"]

    with _FB("su_t") as bk:
        @bk.tensor
        def _(te):
            WT(te, "D", snap["d2"])
            WT(te, "P", snap["p1"])
            mm = None
            for e in range(NE):
                mm = te.transpose(PS1[:, e * 32:(e + 1) * 32],
                                  stage[0:32, 1024 + e * 128:1024 + (e + 1) * 128], identF[0:32, 0:32])
            for kt in range(NKT):
                mm = te.transpose(PS2[:, kt * 8:(kt + 1) * 8],
                                  stage[0:8, 2048 + kt * 128:2048 + (kt + 1) * 128], identF[0:8, 0:8])
            IC(te.matmul(PO[:, 0:16], ones0[0:1, 0:128], hm_bf[:], start=True, stop=True, **MM), "T", 1)
    snap["sut"] = C["T"]

    with _FB("su_v") as bk:
        @bk.vector
        def _(ve):
            WT(ve, "T", snap["sut"])
            ve.tensor_copy(b1fm[:], PS1[:, 0:256])
            ve.tensor_copy(WgTb[:], PS2[:, 0:64])
            IC(ve.tensor_copy(hm_b[:], PO[:, 0:16]), "V", 1)
            WT(ve, "V", C["V"])
            IC(ve.tensor_tensor(hm2[:], hm_b[:], hm_b[:], op=ALU.mult), "V", 1)
            WT(ve, "V", C["V"])
            IC(ve.tensor_scalar(hm2[:], hm2[:], 0.125, None, op0=ALU.mult), "V", 1)
            WT(ve, "V", C["V"])
            IC(ve.tensor_tensor(qb[0:64, :], qb[0:64, :], hm2[0:64, :], op=ALU.mult), "V", 1)
    snap["suv"] = C["V"]
    seed_rel = [("T", snap["sut"]), ("P", snap["p2"]), ("V", snap["suv"])]

    xcast = [0] * NT
    for t in range(NT):
        with _FB(f"xl{t}") as bk:
            hold = []

            @bk.sync
            def _(sy, t=t):
                hold.append(ring_dma(sy, xin[t * 128:(t + 1) * 128, :], 128, 1024))

            @bk.gpsimd
            def _(pl, t=t):
                ci = hold[0]
                rwait(pl, ci)
                IC(pl.tensor_copy(x16[:, t * 1024:(t + 1) * 1024], slotview(ci, 128, 1024)), "K", 1)
                set_rel(ci, ["K"])
                xcast[t] = C["K"]

    def emit_qkv_chunks(p):
        with _FB(f"qc{p}") as bk:
            @bk.sync
            def _(sy, p=p):
                cis = [ring_dma(sy, qkvw[384 * p + 128 * r:384 * p + 128 * (r + 1), :], 128, 1024)
                       for r in range(3)]
                snap[("qch", p)] = cis

    emit_qkv_chunks(0)

    def emit_layernorm(tag, with_logits):
        def stage_s(t):
            with _FB(f"{tag}s{t}") as bk:
                @bk.scalar
                def _(se, t=t):
                    if tag == "ln1":
                        WT(se, "K", xcast[t])
                    else:
                        WT(se, "V", snap[("fcev", t)])
                    IC(se.activation(sqjunk[:, 0:1024], x16[:, t * 1024:(t + 1) * 1024], AF.Square,
                                     accum_out=sumsq[:, t:t + 1]), "S", 1)
                snap[(f"{tag}sq", t)] = C["S"]

                @bk.vector
                def _(ve, t=t):
                    if tag == "ln1":
                        WT(ve, "K", xcast[t])
                    else:
                        WT(ve, "V", snap[("fcev", t)])
                    IC(ve.reduce_sum(sums[:, t:t + 1], x16[:, t * 1024:(t + 1) * 1024], axis=AX.X), "V", 1)
                snap[(f"{tag}sm", t)] = C["V"]

        def stage_r(t):
            with _FB(f"{tag}r{t}") as bk:
                @bk.vector
                def _(ve, t=t):
                    WT(ve, "V", snap[(f"{tag}sm", t)])
                    ve.tensor_scalar(meanv[:, t:t + 1], sums[:, t:t + 1], 1.0 / D, None, op0=ALU.mult)
                    WT(ve, "S", snap[(f"{tag}sq", t)])
                    ve.tensor_scalar(varv[:, t:t + 1], sumsq[:, t:t + 1], 1.0 / D, None, op0=ALU.mult)
                    IC(ve.tensor_tensor(tmp16[:, t:t + 1], meanv[:, t:t + 1], meanv[:, t:t + 1],
                                        op=ALU.mult), "V", 1)
                    WT(ve, "V", C["V"])
                    ve.tensor_tensor(varv[:, t:t + 1], varv[:, t:t + 1], tmp16[:, t:t + 1], op=ALU.subtract)
                    IC(ve.tensor_scalar(varv[:, t:t + 1], varv[:, t:t + 1], EPS, None, op0=ALU.add), "V", 1)
                snap[(f"{tag}var", t)] = C["V"]

                @bk.scalar
                def _(se, t=t):
                    WT(se, "V", snap[(f"{tag}var", t)])
                    se.activation(tmp16[:, t:t + 1], varv[:, t:t + 1], AF.Ln)
                    IC(se.activation(rstd[:, t:t + 1], tmp16[:, t:t + 1], AF.Exp, scale=-0.5), "S", 1)
                snap[(f"{tag}rstd", t)] = C["S"]

        def stage_n(t):
            with _FB(f"{tag}n{t}") as bk:
                @bk.vector
                def _(ve, t=t):
                    WT(ve, "S", snap[(f"{tag}rstd", t)])
                    IC(ve.scalar_tensor_tensor(negmr[:, t:t + 1], meanv[:, t:t + 1], -1.0,
                                               rstd[:, t:t + 1], op0=ALU.mult, op1=ALU.mult), "V", 1)
                    WT(ve, "V", C["V"])
                    if t >= 2:
                        WT(ve, "T", snap[(f"{tag}tr", t - 2)])
                    IC(ve.tensor_scalar(xtok[t % 2][:, 0:1024], x16[:, t * 1024:(t + 1) * 1024],
                                        rstd[:, t:t + 1], negmr[:, t:t + 1],
                                        op0=ALU.mult, op1=ALU.add), "V", 1)
                snap[(f"{tag}nm", t)] = C["V"]

        def stage_t(t):
            with _FB(f"{tag}t{t}") as bk:
                @bk.tensor
                def _(te, t=t):
                    WT(te, "V", snap[(f"{tag}nm", t)])
                    pwait(te, "PS1a" if t % 2 == 0 else "PS2a")
                    mm = None
                    for kt in range(NKT):
                        mm = te.transpose(pbfLN[t % 2][:, kt * 128:(kt + 1) * 128],
                                          xtok[t % 2][:, kt * 128:(kt + 1) * 128], ident[:])
                    IC(mm, "T", 1)
                snap[(f"{tag}tr", t)] = C["T"]

                @bk.vector
                def _(ve, t=t):
                    WT(ve, "T", snap[(f"{tag}tr", t)])
                    dst = xnfm[:, :].rearrange("p (kt c) -> p kt c", c=2048)[:, :, t * 128:(t + 1) * 128]
                    src = pbfLN[t % 2][:, :].rearrange("p (kt c) -> p kt c", c=128)
                    IC(ve.tensor_copy(dst, src), "V", 1)
                pset("V", "PS1a" if t % 2 == 0 else "PS2a")
                snap[(f"{tag}ev", t)] = C["V"]

        def stage_l(t):
            if with_logits:
                with _FB(f"{tag}l{t}") as bk:
                    @bk.tensor
                    def _(te, t=t):
                        WT(te, "V", snap[(f"{tag}ev", t)])
                        pwait(te, *(("POa0", "POa1") if t % 2 == 0 else ("POb",)))
                        mm = None
                        for kt in range(NKT):
                            mm = te.matmul(PO[:, (t % 2) * 1024 + 0:(t % 2) * 1024 + 8],
                                           xnfm[:, kt * 2048 + t * 128:kt * 2048 + (t + 1) * 128],
                                           WgTb[:, kt * 8:(kt + 1) * 8],
                                           start=(kt == 0), stop=(kt == NKT - 1), **MM)
                        IC(mm, "T", 1)
                    snap[(f"{tag}lm", t)] = C["T"]

                    @bk.vector
                    def _(ve, t=t):
                        WT(ve, "T", snap[(f"{tag}lm", t)])
                        IC(ve.tensor_copy(logits[:, t * 8:(t + 1) * 8],
                                          PO[:, (t % 2) * 1024:(t % 2) * 1024 + 8]), "V", 1)
                    pset("V", *(("POa0", "POa1") if t % 2 == 0 else ("POb",)))
                    snap[(f"{tag}lev", t)] = C["V"]

        stages = [stage_s, stage_r, stage_n, stage_t, stage_l]
        for w in range(NT + len(stages) - 1):
            for si, st in enumerate(stages):
                t = w - si
                if 0 <= t < NT:
                    st(t)
        snap[f"{tag}sq"] = C["S"]

    emit_layernorm("ln1", with_logits=False)
    if _DEBUG:
        with _FB("dbg_xnfm1") as bk:
            @bk.sync
            def _(sy):
                for (s, v) in [("V", C["V"])]:
                    WT(sy, s, v)
                IC(sy.dma_start(out=dbg["xnfm1"], in_=xnfm[:, :].bitcast(mybir.dt.uint16)), "B", 16)

    QKV_DST = {(0, 0): 0, (0, 1): 128, (1, 0): 256, (1, 1): 64, (2, 0): 192, (2, 1): 320}

    def emit_qkv_transpose_r(p, r):
        buf = qkvT[p % 2]
        cis = snap[("qch", p)]
        if True:
            with _FB(f"qt{p}_{r}") as bk:
                @bk.tensor
                def _(te, r=r, p=p):
                    ci = cis[r]
                    rwait(te, ci)
                    if p >= 2:
                        WT(te, "T", snap[("qkdone", p - 2)])
                    regs = ("PS1a", "PS1b") if r % 2 == 0 else ("PS2a", "PS2b")
                    pwait(te, *regs)
                    ps = PS1 if r % 2 == 0 else PS2
                    mm = None
                    for kt in range(NKT):
                        mm = te.transpose(ps[:, kt * 128:(kt + 1) * 128],
                                          slotview(ci, 128, 1024)[:, kt * 128:(kt + 1) * 128],
                                          identF[:])
                    IC(mm, "T", 1)
                    set_rel(ci, ["T"])
                snap[("qtr", p, r)] = C["T"]

                @bk.vector
                def _(ve, r=r, p=p):
                    WT(ve, "T", snap[("qtr", p, r)])
                    ps = PS1 if r % 2 == 0 else PS2
                    last = None
                    for half in range(2):
                        src = ps[:, :].rearrange("p (kt c) -> p kt c", c=128)[:, :, half * 64:(half + 1) * 64]
                        do = QKV_DST[(r, half)]
                        dst = buf.rearrange("p (kt c) -> p kt c", c=384)[:, :, do:do + 64]
                        last = ve.tensor_copy(dst, src)
                    IC(last, "V", 1)
                regs = ("PS1a", "PS1b") if r % 2 == 0 else ("PS2a", "PS2b")
                pset("V", *regs)
        if r == 2:
            snap[("qkv_rdy", p)] = C["V"]

    for _r in range(3):
        emit_qkv_transpose_r(0, _r)
    emit_qkv_chunks(1)
    snap["fcch"] = [None] * 8

    def emit_fc_chunk(et):
        with _FB(f"fcc{et}") as bk:
            @bk.sync
            def _(sy, et=et):
                snap["fcch"][et] = ring_dma(sy, fcw[et * 128:(et + 1) * 128, :], 128, 1024)

    def emit_Av(h, tt):
        p, hi = divmod(h, 2)
        vb = (p % 2) * 2
        with _FB(f"h{h}v{tt}") as bk:
            reg = "PS2a" if tt % 2 == 0 else "PS2b"

            @bk.tensor
            def _(te, tt=tt, p=p, reg=reg):
                if tt == 0:
                    WT(te, "V", snap[("qkv_rdy", p)])
                pwait(te, reg)
                pv = PS2[:, (tt % 2) * 512:(tt % 2) * 512 + 128]
                for kt in range(NKT):
                    te.matmul(pv, xnfm[:, kt * 2048 + tt * 128:kt * 2048 + (tt + 1) * 128],
                              qkvT[p % 2][:, kt * 384 + 256:kt * 384 + 384],
                              start=(kt == 0), stop=False, **MM)
                te.matmul(pv[:, 0:64], ones0[0:1, 0:128],
                          vb_row[0:1, (2 * p) * 64:(2 * p) * 64 + 64], start=False, stop=False, **MM)
                IC(te.matmul(pv[:, 64:128], ones0[0:1, 0:128],
                             vb_row[0:1, (2 * p + 1) * 64:(2 * p + 1) * 64 + 64],
                             start=False, stop=True, **MM), "T", 1)
            snap[("vmm", p, tt)] = C["T"]

            @bk.vector
            def _(ve, tt=tt, p=p, vb=vb):
                WT(ve, "T", snap[("vmm", p, tt)])
                if tt == 0 and p >= 2:
                    WT(ve, "T", snap[("avdone", 2 * p - 3)])
                pv = PS2[:, (tt % 2) * 512:(tt % 2) * 512 + 128]
                ve.tensor_scalar(vsb4[vb][:, tt * 65:tt * 65 + 64], pv[:, 0:64],
                                 hm_b[:, 2 * p:2 * p + 1], None, op0=ALU.mult)
                IC(ve.tensor_scalar(vsb4[vb + 1][:, tt * 65:tt * 65 + 64], pv[:, 64:128],
                                    hm_b[:, 2 * p + 1:2 * p + 2], None, op0=ALU.mult), "V", 1)
            pset("V", reg)
        if tt == NT - 1:
            snap[("vrdy", p)] = C["V"]

    def emit_Aqk(h, c):
        p, hi = divmod(h, 2)
        with _FB(f"h{h}qk{c}") as bk:
            regq = "PS1a" if c % 2 == 0 else "PS1b"
            regk = "PS2a" if c % 2 == 0 else "PS2b"

            @bk.tensor
            def _(te, c=c, h=h, hi=hi, p=p, regq=regq, regk=regk):
                pwait(te, regq, regk)
                pq = PS1[0:64, (c % 2) * 512:(c % 2) * 512 + 512]
                pk = PS2[0:64, (c % 2) * 512:(c % 2) * 512 + 512]
                for kt in range(NKT):
                    te.matmul(pq, qkvT[p % 2][:, kt * 384 + hi * 64:kt * 384 + hi * 64 + 64],
                              xnfm[:, kt * 2048 + c * 512:kt * 2048 + (c + 1) * 512],
                              start=(kt == 0), stop=(kt == NKT - 1), **MM)
                mm = None
                for kt in range(NKT):
                    mm = te.matmul(pk, qkvT[p % 2][:, kt * 384 + 128 + hi * 64:kt * 384 + 128 + hi * 64 + 64],
                                   xnfm[:, kt * 2048 + c * 512:kt * 2048 + (c + 1) * 512],
                                   start=(kt == 0), stop=(kt == NKT - 1), **MM)
                IC(mm, "T", 1)
            snap[("qkmm", h, c)] = C["T"]
            if c == 3 and hi == 1:
                snap[("qkdone", p)] = C["T"]

            @bk.vector
            def _(ve, c=c, h=h, regq=regq, regk=regk):
                WT(ve, "T", snap[("qkmm", h, c)])
                if c == 0 and h >= 2:
                    WT(ve, "T", snap[("etdone", h - 2)])
                if _DEBUG and h == 2 and c == 0:
                    WT(ve, "B", snap["dbgqk"])
                if c == 0:
                    ve.memset(qha2[h % 2][64:65, :], 0.0)
                pq = PS1[0:64, (c % 2) * 512:(c % 2) * 512 + 512]
                pk = PS2[0:64, (c % 2) * 512:(c % 2) * 512 + 512]
                ve.tensor_scalar(qha2[h % 2][0:64, c * 512:(c + 1) * 512], pq,
                                 hm2[0:64, h:h + 1], qb[0:64, h:h + 1],
                                 op0=ALU.mult, op1=ALU.add)
                IC(ve.tensor_scalar(kha2[h % 2][0:64, c * 512:(c + 1) * 512], pk,
                                    kb[0:64, h:h + 1], None, op0=ALU.add), "V", 1)
            pset("V", regq, regk)
            snap[("qkev", h, c)] = C["V"]

    def emit_Be(h, Q, j):
        qt = 8 * Q + j
        with _FB(f"h{h}e{qt}a") as bk:
            @bk.tensor
            def _(te, h=h, Q=Q, j=j, qt=qt):
                if j == 0 and Q == 0:
                    WT(te, "V", snap[("qkev", h, 3)])
                for kq in range(2):
                    if qt >= 1:
                        WT(te, "V", snap[("nmq", h, qt - 1, kq)])
                    else:
                        pwait(te, PSQR[kq])
                    IC(te.matmul(PSQ[kq],
                                 qha2[h % 2][0:128, qt * 128:(qt + 1) * 128],
                                 kha2[h % 2][0:128, kq * 512:(kq + 1) * 512],
                                 start=True, stop=True, **MM), "T", 1)
                    snap[("eq", h, qt, kq)] = C["T"]
                if j >= 1:
                    WT(te, "V", snap[("scr", h, qt - 1)])
                    IC(te.transpose(nt_ps[0], scrT[:, 0:65], ident[:]), "T", 1)
                    snap[("ntt", h, qt - 1)] = C["T"]

            @bk.vector
            def _(ve, h=h, j=j, qt=qt):
                WT(ve, "T", snap[("eq", h, qt, 0)])
                IC(ve.reduce_max(nm2[:, 0:1], PSQ[0], axis=AX.X), "V", 1)
                snap[("nmq", h, qt, 0)] = C["V"]
                if j >= 1:
                    WT(ve, "T", snap[("ntt", h, qt - 1)])
                    IC(ve.tensor_copy(qha2[h % 2][64:65, (qt - 1) * 128:qt * 128],
                                      nt_ps[0][64:65, 0:128]), "V", 1)
                    snap[("rowev", h, qt - 1)] = C["V"]
                WT(ve, "T", snap[("eq", h, qt, 1)])
                IC(ve.reduce_max(nm2[:, 1:2], PSQ[1], axis=AX.X), "V", 1)
                snap[("nmq", h, qt, 1)] = C["V"]

        with _FB(f"h{h}e{qt}b") as bk:
            @bk.tensor
            def _(te, h=h, qt=qt):
                for kq in range(2, 4):
                    if qt >= 1:
                        WT(te, "V", snap[("nmq", h, qt - 1, kq)])
                    else:
                        pwait(te, PSQR[kq])
                    if kq == 3 and qt >= 1:
                        WT(te, "V", snap[("rowev", h, qt - 1)])
                    IC(te.matmul(PSQ[kq],
                                 qha2[h % 2][0:128, qt * 128:(qt + 1) * 128],
                                 kha2[h % 2][0:128, kq * 512:(kq + 1) * 512],
                                 start=True, stop=True, **MM), "T", 1)
                    snap[("eq", h, qt, kq)] = C["T"]

            @bk.vector
            def _(ve, h=h, qt=qt):
                for kq in range(2, 4):
                    WT(ve, "T", snap[("eq", h, qt, kq)])
                    IC(ve.reduce_max(nm2[:, kq:kq + 1], PSQ[kq], axis=AX.X), "V", 1)
                    snap[("nmq", h, qt, kq)] = C["V"]
                WT(ve, "V", C["V"])
                if qt >= 1:
                    WT(ve, "T", snap[("ntt", h, qt - 1)])
                IC(ve.reduce_max(scrT[:, 64:65], nm2[:, 0:4], axis=AX.X, negate=True), "V", 1)
                snap[("scr", h, qt)] = C["V"]

    def emit_Bend(h, Q):
        qt = 8 * Q + 7
        with _FB(f"h{h}eend{Q}") as bk:
            @bk.tensor
            def _(te, h=h, qt=qt):
                WT(te, "V", snap[("scr", h, qt)])
                IC(te.transpose(nt_ps[0], scrT[:, 0:65], ident[:]), "T", 1)
            snap[("ntt", h, qt)] = C["T"]

            @bk.vector
            def _(ve, h=h, qt=qt):
                WT(ve, "T", snap[("ntt", h, qt)])
                IC(ve.tensor_copy(qha2[h % 2][64:65, qt * 128:(qt + 1) * 128],
                                  nt_ps[0][64:65, 0:128]), "V", 1)
            snap[("rowev", h, qt)] = C["V"]
            snap[("rowdone", h, Q)] = C["V"]
            pset("V", "PS1a", "PS1b", "PS2a", "PS2b")

    def emit_Cct(h, Q, ct):
        p, hi = divmod(h, 2)
        vb = (p % 2) * 2 + hi
        with _FB(f"h{h}s{Q}_{ct}") as bk:
            @bk.tensor
            def _(te, h=h, Q=Q, ct=ct, vb=vb, p=p):
                if ct == 0:
                    WT(te, "V", snap[("rowdone", h, Q)])
                    WT(te, "V", snap[("vrdy", p)])
                    pwait(te, "POa0", "POa1", "POb")
                for v in range(2):
                    if ct >= 1:
                        WT(te, "S", snap[("expq", h, Q, v, ct - 1)])
                        te.matmul(PO[0:65, 1024 + v * 512:1024 + (v + 1) * 512],
                                  vsb4[vb][:, (ct - 1) * 65:ct * 65],
                                  PuT[:, v * 512:(v + 1) * 512],
                                  start=(ct - 1 == 0), stop=False, **MM)
                    mm = te.matmul(PO[:, v * 512:(v + 1) * 512],
                                   kha2[h % 2][0:128, ct * 128:(ct + 1) * 128],
                                   qha2[h % 2][0:128, Q * 1024 + v * 512:Q * 1024 + (v + 1) * 512],
                                   start=True, stop=True, **MM)
                    IC(mm, "T", 1)
                    snap[("etq", h, Q, v, ct)] = C["T"]

            @bk.scalar
            def _(se, h=h, Q=Q, ct=ct):
                for v in range(2):
                    WT(se, "T", snap[("etq", h, Q, v, ct)])
                    IC(se.activation(PuT[:, v * 512:(v + 1) * 512],
                                     PO[:, v * 512:(v + 1) * 512], AF.Exp), "S", 1)
                    snap[("expq", h, Q, v, ct)] = C["S"]

    def emit_Clast(h, Q):
        p, hi = divmod(h, 2)
        vb = (p % 2) * 2 + hi
        with _FB(f"h{h}sl{Q}") as bk:
            @bk.tensor
            def _(te, h=h, Q=Q, vb=vb):
                ct = NT - 1
                mm = None
                for v in range(2):
                    WT(te, "S", snap[("expq", h, Q, v, ct)])
                    mm = te.matmul(PO[0:65, 1024 + v * 512:1024 + (v + 1) * 512],
                                   vsb4[vb][:, ct * 65:(ct + 1) * 65],
                                   PuT[:, v * 512:(v + 1) * 512],
                                   start=False, stop=True, **MM)
                IC(mm, "T", 1)
            snap[("otdone", h, Q)] = C["T"]
            if Q == 1:
                snap[("etdone", h)] = C["T"]
                snap[("avdone", h)] = C["T"]

    def emit_H4(h, Q):
        p, hi = divmod(h, 2)
        with _FB(f"h{h}n{Q}") as bk:
            @bk.scalar
            def _(se, h=h, Q=Q):
                WT(se, "T", snap[("otdone", h, Q)])
                se.activation(lnr[64:65, 0:1024], PO[64:65, 1024:2048], AF.Ln)
                IC(se.activation(recipr[0:1, 0:1024], lnr[64:65, 0:1024],
                                 AF.Exp, scale=-1.0), "S", 1)
            snap[("rec", h, Q)] = C["S"]

            @bk.tensor
            def _(te, h=h, Q=Q):
                WT(te, "S", snap[("rec", h, Q)])
                mm = None
                for v in range(2):
                    mm = te.matmul(PO[:, v * 512:(v + 1) * 512], ones64[0:1, 0:128],
                                   recipr[0:1, v * 512:(v + 1) * 512], start=True, stop=True, **MM)
                IC(mm, "T", 1)
            snap[("bch", h, Q)] = C["T"]

            @bk.vector
            def _(ve, h=h, Q=Q, hi=hi, p=p):
                WT(ve, "T", snap[("bch", h, Q)])
                ve.tensor_copy(bcast[:, 0:512], PO[:, 0:512])
                IC(ve.tensor_copy(bcast[:, 512:1024], PO[:, 512:1024]), "V", 1)
                WT(ve, "V", C["V"])
                if hi == 0:
                    dst = attn_fm(p)[0:64, Q * 1024:(Q + 1) * 1024]
                else:
                    if h >= 3 and Q == 0:
                        WT(ve, "B", snap[("oddout", h - 2)])
                    dst = tmpodd[0:64, Q * 1024:(Q + 1) * 1024]
                IC(ve.tensor_tensor(dst, PO[0:64, 1024:2048], bcast[0:64, 0:1024],
                                    op=ALU.mult), "V", 1)
            snap[("mul", h, Q)] = C["V"]
            if Q == 1:
                snap[("mul", h)] = C["V"]
            pset("V", "POa0", "POa1", "POb")

        if hi == 1 and Q == 1:
            with _FB(f"h{h}n3") as bk:
                @bk.sync
                def _(sy, h=h, p=p):
                    WT(sy, "V", snap[("mul", h)])
                    IC(sy.dma_start(out=oddb, in_=tmpodd[0:64, :]), "B", 16)
                    snap[("oddout", h)] = C["B"]
                    WT(sy, "B", C["B"])
                    IC(sy.dma_start(out=attn_fm(p)[64:128, :], in_=oddb), "B", 16)
                snap[("oddin", h)] = C["B"]

    def lane_items_A_B0(h):
        items = []
        if h % 2 == 0:
            items += [("Av", h, tt) for tt in range(NT)]
        items += [("Aqk", h, c) for c in range(4)]
        items += [("Be", h, 0, j) for j in range(8)] + [("Bend", h, 0)]
        return items

    def lane_items_B1(h):
        p, hi = divmod(h, 2)
        items = [("Be", h, 1, j) for j in range(8)] + [("Bend", h, 1)]
        if hi == 1:
            if p + 1 < NH // 2:
                items += [("QT", p + 1, r) for r in range(3)]
            if p + 2 < NH // 2:
                items += [("QC", p + 2)]
            if p == NH // 2 - 2:
                items += [("FCC", et) for et in range(3)]
        return items

    def lane_items_C(h, Q):
        return ([("Cct", h, Q, ct) for ct in range(NT)] +
                [("Clast", h, Q), ("H4", h, Q)])

    def emit_item(it):
        kind = it[0]
        if kind == "Av":
            emit_Av(it[1], it[2])
        elif kind == "Aqk":
            emit_Aqk(it[1], it[2])
        elif kind == "Be":
            emit_Be(it[1], it[2], it[3])
        elif kind == "Bend":
            emit_Bend(it[1], it[2])
        elif kind == "Cct":
            emit_Cct(it[1], it[2], it[3])
        elif kind == "Clast":
            emit_Clast(it[1], it[2])
        elif kind == "H4":
            emit_H4(it[1], it[2])
        elif kind == "QT":
            emit_qkv_transpose_r(it[1], it[2])
        elif kind == "QC":
            emit_qkv_chunks(it[1])
        elif kind == "FCC":
            emit_fc_chunk(it[1])

    def emit_interleaved(l1, l2):
        n1, n2 = len(l1), len(l2)
        i1 = i2 = 0
        while i1 < n1 or i2 < n2:
            if i2 >= n2 or (i1 < n1 and i1 * max(n2, 1) < i2 * max(n1, 1)):
                emit_item(l1[i1])
                i1 += 1
            else:
                emit_item(l2[i2])
                i2 += 1

    prev_tail = []
    for h in range(NH):
        emit_interleaved(prev_tail, lane_items_A_B0(h))
        emit_interleaved(lane_items_C(h, 0), lane_items_B1(h))
        prev_tail = lane_items_C(h, 1)
        if _DEBUG and h == 0:
            with _FB("dbg_qk") as bk:
                @bk.sync
                def _(sy):
                    WT(sy, "V", snap[("rowdone", 0, 1)])
                    U16 = mybir.dt.uint16
                    IC(sy.dma_start(out=dbg["const"][:, 0:16], in_=hm_b[:, :]), "B", 16)
                    IC(sy.dma_start(out=dbg["const"][:, 16:32], in_=hm2[:, :]), "B", 16)
                    IC(sy.dma_start(out=dbg["const"][:, 32:48], in_=qb[:, :]), "B", 16)
                    IC(sy.dma_start(out=dbg["const"][:, 48:64], in_=kb[:, :]), "B", 16)
                    IC(sy.dma_start(out=dbg["qha"], in_=qha2[0][:, :].bitcast(U16)), "B", 16)
                    IC(sy.dma_start(out=dbg["kha"], in_=kha2[0][:, :].bitcast(U16)), "B", 16)
                    IC(sy.dma_start(out=dbg["vsb"], in_=vsb4[0][:, :].bitcast(U16)), "B", 16)
            snap["dbgqk"] = C["B"]
    for it in prev_tail:
        emit_item(it)
    if _DEBUG:
        with _FB("dbg_afm") as bk:
            @bk.sync
            def _(sy):
                WT(sy, "V", snap[("mul", 0)])
                WT(sy, "B", snap[("oddin", 1)])
                IC(sy.dma_start(out=dbg["afm"], in_=attn_fm(0)[:, :].bitcast(mybir.dt.uint16)), "B", 16)


    fcbf = [xtok[0], xtok[1]]
    PS1bf_full = PS1[:, 0:512].bitcast(BF16)
    for et in range(8):
        if et >= 3:
            emit_fc_chunk(et)
        with _FB(f"fcp{et}") as bk:
            @bk.gpsimd
            def _(pl, et=et):
                ci = snap["fcch"][et]
                rwait(pl, ci)
                if et >= 2:
                    WT(pl, "T", snap[("fctr", et - 2)])
                IC(pl.tensor_copy(fcbf[et % 2][:, 0:1024], slotview(ci, 128, 1024)), "K", 1)
                set_rel(ci, ["K"])
            snap[("fcc", et)] = C["K"]

        with _FB(f"fct{et}") as bk:
            @bk.tensor
            def _(te, et=et):
                WT(te, "K", snap[("fcc", et)])
                pwait(te, "PS1a")
                mm = None
                for h16 in range(16):
                    mm = te.transpose(PS1bf_full[64 * (h16 % 2):64 * (h16 % 2) + 64,
                                                 128 * (h16 // 2):128 * (h16 // 2) + 128],
                                      fcbf[et % 2][:, h16 * 64:(h16 + 1) * 64],
                                      ident[:], tile_position=(0, 64 * (h16 % 2)))
                IC(mm, "T", 1)
            snap[("fctr", et)] = C["T"]

            @bk.vector
            def _(ve, et=et):
                WT(ve, "T", snap[("fctr", et)])
                last = None
                for half in range(2):
                    src = PS1bf_full[64 * half:64 * half + 64, :].rearrange("p (b c) -> p b c", c=128)
                    dst = fcT[64 * half:64 * half + 64, :].rearrange("p (b c) -> p b c", c=1024)[:, :, et * 128:(et + 1) * 128]
                    last = ve.tensor_copy(dst, src)
                IC(last, "V", 1)
            pset("V", "PS1a")
    snap["fcT_rdy"] = C["V"]

    for qt in range(NT):
        with _FB(f"fc{qt}") as bk:
            @bk.tensor
            def _(te, qt=qt):
                if qt == 0:
                    WT(te, "V", snap["fcT_rdy"])
                    for hh in range(0, NH, 2):
                        WT(te, "V", snap[("mul", hh)])
                    for hh in range(1, NH, 2):
                        WT(te, "B", snap[("oddin", hh)])
                regsE = ("PS1a", "PS1b") if qt % 2 == 0 else ("POa0", "POa1")
                pwait(te, *regsE)
                dstE = PS1[:, :] if qt % 2 == 0 else PO[:, 0:1024]
                for pp in range(8):
                    for dh in range(2):
                        te.matmul(dstE[:, dh * 512:(dh + 1) * 512],
                                  attn_fm(pp)[0:128, qt * 128:(qt + 1) * 128],
                                  fcT[0:128, pp * 1024 + dh * 512:pp * 1024 + (dh + 1) * 512],
                                  start=(pp == 0), stop=False, **MM)
                mm = None
                for dh in range(2):
                    mm = te.matmul(dstE[:, dh * 512:(dh + 1) * 512], ones0[0:1, 0:128],
                                   fcb_row[0:1, dh * 512:(dh + 1) * 512], start=False, stop=True, **MM)
                IC(mm, "T", 1)
            snap[("fcmm", qt)] = C["T"]

            @bk.vector
            def _(ve, qt=qt):
                WT(ve, "T", snap[("fcmm", qt)])
                dstE = PS1[:, :] if qt % 2 == 0 else PO[:, 0:1024]
                xt = x16[:, qt * 1024:(qt + 1) * 1024]
                IC(ve.tensor_tensor(xt, dstE, xt, op=ALU.add), "V", 1)
            if qt % 2 == 0:
                pset("V", "PS1a", "PS1b")
            else:
                pset("V", "POa0", "POa1")
            snap[("fcev", qt)] = C["V"]
    snap["fc_done_t"] = C["T"]
    snap["fc_done_v"] = C["V"]

    if _DEBUG:
        with _FB("dbg_x16a") as bk:
            @bk.sync
            def _(sy):
                WT(sy, "V", C["V"])
                IC(sy.dma_start(out=dbg["x16a"], in_=x16[:, :].bitcast(mybir.dt.uint16)), "B", 16)

    emit_layernorm("ln2", with_logits=True)
    snap["ln2_done"] = C["V"]

    with _FB("gat1") as bk:
        @bk.vector
        def _(ve):
            last = None
            for t in range(NT):
                WT(ve, "V", snap[("ln2lev", t)])
                last = ve.reduce_max(maxs1[:, t:t + 1], logits[:, t * 8:(t + 1) * 8], axis=AX.X)
            IC(last, "V", 1)
            WT(ve, "V", C["V"])
            for t in range(NT):
                ve.tensor_scalar(masks1[:, t * 8:(t + 1) * 8], logits[:, t * 8:(t + 1) * 8],
                                 maxs1[:, t:t + 1], None, op0=ALU.is_ge)
            for t in range(NT):
                ve.scalar_tensor_tensor(gt2[:, (t % 2) * 8:(t % 2) * 8 + 8],
                                        masks1[:, t * 8:(t + 1) * 8], -1e30,
                                        logits[:, t * 8:(t + 1) * 8], op0=ALU.mult, op1=ALU.add)
                if t >= 1:
                    ve.reduce_max(maxs2[:, t - 1:t], gt2[:, ((t - 1) % 2) * 8:((t - 1) % 2) * 8 + 8], axis=AX.X)
            IC(ve.reduce_max(maxs2[:, NT - 1:NT], gt2[:, ((NT - 1) % 2) * 8:((NT - 1) % 2) * 8 + 8], axis=AX.X), "V", 1)
            WT(ve, "V", C["V"])
            for t in range(NT):
                ve.scalar_tensor_tensor(gt2[:, (t % 2) * 8:(t % 2) * 8 + 8],
                                        masks1[:, t * 8:(t + 1) * 8], -1e30,
                                        logits[:, t * 8:(t + 1) * 8], op0=ALU.mult, op1=ALU.add)
                if t >= 1:
                    ve.tensor_scalar(masks2[:, (t - 1) * 8:t * 8], gt2[:, ((t - 1) % 2) * 8:((t - 1) % 2) * 8 + 8],
                                     maxs2[:, t - 1:t], None, op0=ALU.is_ge)
            ve.tensor_scalar(masks2[:, (NT - 1) * 8:NT * 8], gt2[:, ((NT - 1) % 2) * 8:((NT - 1) % 2) * 8 + 8],
                             maxs2[:, NT - 1:NT], None, op0=ALU.is_ge)
            IC(ve.tensor_tensor(tmp16[:], maxs2[:], maxs1[:], op=ALU.subtract), "V", 1)
    snap["gd"] = C["V"]

    with _FB("gat2") as bk:
        @bk.scalar
        def _(se):
            WT(se, "V", snap["gd"])
            IC(se.activation(e2t[:], tmp16[:], AF.Exp), "S", 1)
    snap["ge"] = C["S"]

    with _FB("gat3") as bk:
        @bk.vector
        def _(ve):
            WT(ve, "S", snap["ge"])
            IC(ve.tensor_scalar(tmp16[:], e2t[:], 1.0, None, op0=ALU.add), "V", 1)
            WT(ve, "V", C["V"])
            IC(ve.reciprocal(rden[:], tmp16[:]), "V", 1)
            WT(ve, "V", C["V"])
            IC(ve.tensor_tensor(w2g[:], e2t[:], rden[:], op=ALU.mult), "V", 1)
            WT(ve, "V", C["V"])
            for t in range(NT):
                ve.tensor_scalar(gates[:, t * 8:(t + 1) * 8], masks1[:, t * 8:(t + 1) * 8],
                                 rden[:, t:t + 1], None, op0=ALU.mult)
            last = None
            for t in range(NT):
                last = ve.scalar_tensor_tensor(gates[:, t * 8:(t + 1) * 8], masks2[:, t * 8:(t + 1) * 8],
                                               w2g[:, t:t + 1], gates[:, t * 8:(t + 1) * 8],
                                               op0=ALU.mult, op1=ALU.add)
            IC(last, "V", 1)
    snap["gates_done"] = C["V"]

    if _DEBUG:
        with _FB("dbg_g") as bk:
            @bk.sync
            def _(sy):
                WT(sy, "V", snap["gates_done"])
                IC(sy.dma_start(out=dbg["gates"], in_=gates[:, :]), "B", 16)

    def emit_moe_chunks(egi):
        e, g = divmod(egi, 4)
        par = egi % 2
        chunks = []
        if egi == 0:
            w1_free = [("V", snap["fc_done_v"])]
            w2_free = [("V", snap["fc_done_v"])]
        elif egi == 1:
            w1_free = [("T", snap[("ln2tr", NT - 1)]), ("S", snap["ln2sq"]),
                       ("V", snap[("ln2nm", NT - 1)])]
            w2_free = [("V", snap["fc_done_v"])]
        else:
            w1_free = [("T", snap[("hdone", egi - 2)])]
            w2_free = [("T", snap[("odone", egi - 2)])]
        for kt in range(NKT):
            chunks.append((w1[e, kt * 128:(kt + 1) * 128, g * 1024:(g + 1) * 1024], 128, 1024,
                           w1v[par][:, kt * 1024:(kt + 1) * 1024],
                           w1_free if kt == 0 else None))
        for itl in range(8):
            chunks.append((w2[e, (g * 8 + itl) * 128:(g * 8 + itl + 1) * 128, :], 128, 1024,
                           w2v[par][:, itl * 1024:(itl + 1) * 1024],
                           w2_free if itl == 0 else None))
        if g == 1:
            chunks.append((b2[e:e + 1, :], 1, 1024, b2_row[:],
                           [("T", snap[("odone", egi - 2)])] if egi >= 2 else [("V", snap["fc_done_v"])]))
        for i, (src, pp, nn, dst, dwait) in enumerate(chunks):
            with _FB(f"mc{egi}_{i}") as bk:
                hold = []

                @bk.sync
                def _(sy, src=src, pp=pp, nn=nn):
                    hold.append(ring_dma(sy, src, pp, nn))

                @bk.gpsimd
                def _(pl, pp=pp, nn=nn, dst=dst, dwait=dwait):
                    ci = hold[0]
                    rwait(pl, ci)
                    if dwait is not None:
                        for (ds, dv) in dwait:
                            WT(pl, ds, dv)
                    IC(pl.tensor_copy(dst, slotview(ci, pp, nn)), "K", 1)
                    set_rel(ci, ["K"])
        snap[("wk", egi)] = C["K"]

    emit_moe_chunks(0)

    for egi in range(NE * 4):
        e, g = divmod(egi, 4)
        par = egi % 2
        if egi + 1 < NE * 4:
            emit_moe_chunks(egi + 1)
        for itl in range(8):
            with _FB(f"m{egi}h{itl}") as bk:
                @bk.tensor
                def _(te, egi=egi, e=e, g=g, par=par, itl=itl):
                    if itl == 0:
                        WT(te, "K", snap[("wk", egi)])
                        if egi == 0:
                            WT(te, "V", snap[("ln2ev", NT - 1)])
                    for tq in range(4):
                        if itl == 0:
                            pwait(te, PSQR[tq])
                        else:
                            WT(te, "S", snap[("gel", egi, itl - 1, tq)])
                        mm = None
                        for kt in range(NKT):
                            mm = te.matmul(PSQ[tq], w1v[par][:, kt * 1024 + itl * 128:kt * 1024 + (itl + 1) * 128],
                                           xnfm[:, kt * 2048 + tq * 512:kt * 2048 + (tq + 1) * 512],
                                           start=(kt == 0), stop=(kt == NKT - 1), **MM)
                        IC(mm, "T", 1)
                        snap[("hmm", egi, itl, tq)] = C["T"]

                @bk.scalar
                def _(se, egi=egi, e=e, g=g, itl=itl):
                    if egi == 0 and itl == 0:
                        WT(se, "T", snap["fc_done_t"])
                    for tq in range(4):
                        WT(se, "T", snap[("hmm", egi, itl, tq)])
                        IC(se.activation(h_g[:, itl * 2048 + tq * 512:itl * 2048 + (tq + 1) * 512],
                                         PSQ[tq], AF.Gelu,
                                         bias=b1fm[:, e * 32 + g * 8 + itl:e * 32 + g * 8 + itl + 1]), "S", 1)
                        snap[("gel", egi, itl, tq)] = C["S"]
                        pset("S", PSQR[tq])
        snap[("hdone", egi)] = C["T"]

        for tc in range(NT):
            with _FB(f"m{egi}o{tc}") as bk:
                @bk.tensor
                def _(te, egi=egi, e=e, g=g, par=par, tc=tc):
                    if tc == 0:
                        for tq in range(4):
                            WT(te, "S", snap[("gel", egi, 7, tq)])
                    if tc % 2 == 0:
                        pwait(te, "POa0", "POa1")
                    else:
                        pwait(te, "POb")
                    base = (tc % 2) * 1024
                    mm = None
                    for itl in range(8):
                        for dh in range(2):
                            last = (itl == 7 and g != 3)
                            mm = te.matmul(PO[:, base + dh * 512:base + (dh + 1) * 512],
                                           h_g[:, itl * 2048 + tc * 128:itl * 2048 + (tc + 1) * 128],
                                           w2v[par][:, itl * 1024 + dh * 512:itl * 1024 + (dh + 1) * 512],
                                           start=(itl == 0), stop=last, **MM)
                    if g == 3:
                        for dh in range(2):
                            mm = te.matmul(PO[:, base + dh * 512:base + (dh + 1) * 512],
                                           ones0[0:1, 0:128], b2_row[0:1, dh * 512:(dh + 1) * 512],
                                           start=False, stop=True, **MM)
                    IC(mm, "T", 1)
                snap[("omm", egi, tc)] = C["T"]

                @bk.vector
                def _(ve, egi=egi, e=e, tc=tc):
                    WT(ve, "T", snap[("omm", egi, tc)])
                    if egi == 0 and tc == 0:
                        WT(ve, "V", snap["gates_done"])
                    base = (tc % 2) * 1024
                    xt = x16[:, tc * 1024:(tc + 1) * 1024]
                    IC(ve.scalar_tensor_tensor(xt, PO[:, base:base + 1024],
                                               gates[:, tc * 8 + e:tc * 8 + e + 1], xt,
                                               op0=ALU.mult, op1=ALU.add), "V", 1)
                pset("V", *(("POa0", "POa1") if tc % 2 == 0 else ("POb",)))
                snap[("oev", egi, tc)] = C["V"]
        snap[("odone", egi)] = C["T"]

    last = NE * 4 - 1
    for t in range(NT):
        with _FB(f"out{t}") as bk:
            @bk.vector
            def _(ve, t=t):
                WT(ve, "V", snap[("oev", last, t)])
                if t == 0:
                    WT(ve, "K", C["K"])
                if t >= 3:
                    WT(ve, "Q%d" % (t % 3), 16 * (t // 3))
                sl = t % 3
                IC(ve.tensor_copy(stage[:, sl * 1024:(sl + 1) * 1024], x16[:, t * 1024:(t + 1) * 1024]), "V", 1)
            snap[("outc", t)] = C["V"]

            @bk.sync
            def _(sy, t=t):
                WT(sy, "V", snap[("outc", t)])
                sl = t % 3
                IC(sy.dma_start(out=xout[t * 128:(t + 1) * 128, :], in_=stage[:, sl * 1024:(sl + 1) * 1024]), "Q%d" % (t % 3), 16)

    with _FB("fin") as bk:
        @bk.sync
        def _(sy):
            WT(sy, "Q0", C["Q0"])
            WT(sy, "Q1", C["Q1"])
            WT(sy, "Q2", C["Q2"])
            WT(sy, "B", C["B"])

    assert 16384 + (212863 - nc.sbuf_bytes_remaining) <= A0 - 512, nc.sbuf_bytes_remaining
    stack.close()
    return nc


_NC_CACHE = None


def kernel(**inputs):
    global _NC_CACHE
    if _NC_CACHE is None:
        _NC_CACHE = build_nc()
    nc = _NC_CACHE
    x = np.ascontiguousarray(inputs["L0_attn_x"], dtype=np.float32)
    common = {
        "hm": np.ascontiguousarray(inputs["L0_attn_head_mask"], dtype=np.float32),
        "qkvw": np.ascontiguousarray(inputs["L0_attn_qkv_w"], dtype=np.float32),
        "qkvb": np.ascontiguousarray(inputs["L0_attn_qkv_b"], dtype=np.float32),
        "fcw": np.ascontiguousarray(inputs["L0_attn_fc_w"], dtype=np.float32),
        "fcb": np.ascontiguousarray(inputs["L0_attn_fc_b"], dtype=np.float32),
        "wg": np.ascontiguousarray(inputs["L0_ffn_Wg"], dtype=np.float32),
        "w1": np.ascontiguousarray(inputs["L0_ffn_W1"], dtype=np.float32),
        "b1": np.ascontiguousarray(inputs["L0_ffn_b1"], dtype=np.float32),
        "w2": np.ascontiguousarray(inputs["L0_ffn_W2"], dtype=np.float32),
        "b2": np.ascontiguousarray(inputs["L0_ffn_b2"], dtype=np.float32),
    }
    in_maps = [dict(common, xin=np.ascontiguousarray(x[c])) for c in range(NCORES)]
    res = run_bass_kernel_spmd(nc, in_maps, core_ids=list(range(NCORES)))
    out = np.stack([np.asarray(res.results[c]["xout"], dtype=np.float32) for c in range(NCORES)], axis=0)
    return out
